# version 22
# baseline (speedup 1.0000x reference)
import numpy as np
import concourse.bass as bass
import concourse.mybir as mybir
from concourse.bass import IndirectOffsetOnAxis
from concourse.bass_utils import run_bass_kernel_spmd

F32 = mybir.dt.float32
I32 = mybir.dt.int32
AF = mybir.ActivationFunctionType
ALU = mybir.AluOpType
AX = mybir.AxisListType
f32 = np.float32



class Tok:
    __slots__ = ("key", "val")

    def __init__(self, key, val):
        self.key = key
        self.val = val


class Step:
    __slots__ = ("eng", "key", "fn", "deps", "inc", "val")

    def __init__(self, eng, key, fn, deps, inc, val):
        self.eng = eng
        self.key = key
        self.fn = fn
        self.deps = deps
        self.inc = inc
        self.val = val


class Prog:

    COMPUTE = ("sync", "vector", "scalar", "tensor", "gpsimd")

    def __init__(self):
        self.steps = []
        self.count = {e: 0 for e in self.COMPUTE}
        self.channels = {}

    def channel(self, name, eng):
        self.channels[name] = eng
        self.count[name] = 0
        return name

    def step(self, eng, fn, deps=(), noinc=False):
        inc = 0 if noinc else 1
        self.count[eng] += inc
        s = Step(eng, eng, fn, [d for d in deps if d is not None], inc,
                 self.count[eng])
        self.steps.append(s)
        return Tok(eng, s.val)

    def dma(self, chan, fn, deps=()):
        eng = self.channels[chan]
        self.count[chan] += 16
        s = Step(eng, chan, fn, [d for d in deps if d is not None], 16,
                 self.count[chan])
        self.steps.append(s)
        return Tok(chan, s.val)

    def wait(self, eng, deps):
        s = Step(eng, eng, None, [d for d in deps if d is not None], 0,
                 self.count[eng])
        self.steps.append(s)
        return Tok(eng, self.count[eng])

    def tok(self, key):
        return Tok(key, self.count[key])

    def alloc_sems(self, es, nc):
        self.sems = {}
        for e in self.COMPUTE:
            self.sems[e] = es.enter_context(nc.semaphore(f"ch_{e}"))
        for c in self.channels:
            self.sems[c] = es.enter_context(nc.semaphore(f"ch_{c}"))

    def replay(self, block):
        sems = self.sems

        def body(engname):
            def run(engobj):
                hwm = {}
                for s in self.steps:
                    if s.eng != engname:
                        continue
                    best = {}
                    for d in s.deps:
                        if d.val > best.get(d.key, 0):
                            best[d.key] = d.val
                    for k2, v2 in best.items():
                        if v2 > hwm.get(k2, 0):
                            engobj.wait_ge(sems[k2], v2)
                            hwm[k2] = v2
                    if s.fn is None:
                        continue
                    inst = s.fn(engobj)
                    if s.inc:
                        inst.then_inc(sems[s.key], s.inc)
            return run

        for e in self.COMPUTE:
            getattr(block, e)(body(e))



BPC = 2
E = 8
KC = 8
D = 1024
CAP = 384
CC = CAP // 128
EPS = 1e-5


def build_nc(T=2048, QT=1024):
    from contextlib import ExitStack

    NT = QT // 128
    NQ = (BPC * T) // QT
    QPB = T // QT
    NL = E * CC
    W5 = 5
    NB = E * CC

    nc = bass.Bass("TRN2", target_bir_lowering=False)

    xh = nc.dram_tensor("x", [BPC, T, D], F32, kind="ExternalInput").ap()
    chh = nc.dram_tensor("c", [BPC, D], F32, kind="ExternalInput").ap()
    gfh = nc.dram_tensor("gfmh", [128, KC * E], F32, kind="ExternalInput").ap()
    wth = nc.dram_tensor("WT", [E, D, D], F32, kind="ExternalInput").ap()
    bbeh = nc.dram_tensor("BBEh", [E, 128, D], F32, kind="ExternalInput").ap()
    wmh = nc.dram_tensor("wmod", [D, 6 * D], F32, kind="ExternalInput").ap()
    trh = nc.dram_tensor("TRIUh", [128, 128], F32, kind="ExternalInput").ap()
    rch = nc.dram_tensor("RECTh", [128, T // 128], F32, kind="ExternalInput").ap()
    idh = nc.dram_tensor("IDNh", [128, 128], F32, kind="ExternalInput").ap()
    onh = nc.dram_tensor("ONESh", [128, 128], F32, kind="ExternalInput").ap()
    suh = nc.dram_tensor("SUTh", [128, 128], F32, kind="ExternalInput").ap()
    ich = nc.dram_tensor("IOTACh", [128, CAP], F32, kind="ExternalInput").ap()
    ibh = nc.dram_tensor("IOTBh", [128, 1], F32, kind="ExternalInput").ap()
    eofh = nc.dram_tensor("EOFFh", [128, E], F32, kind="ExternalInput").ap()
    out = nc.dram_tensor("out", [BPC, T, D], F32, kind="ExternalOutput").ap()
    h1sv = [nc.dram_tensor(f"h1s{j}", [QT, D], F32, kind="Internal").ap()
            for j in range(2)]
    x1sv = [nc.dram_tensor(f"x1s{j}", [QT, D], F32, kind="Internal").ap()
            for j in range(3)]
    yslabv = [nc.dram_tensor(f"ysl{j}", [E * CAP, D], F32, kind="Internal").ap()
              for j in range(2)]

    es = ExitStack()
    sb = lambda n, s, dt=F32: es.enter_context(nc.sbuf_tensor(n, s, dt))
    ps = lambda n, s: es.enter_context(nc.psum_tensor(n, s, F32))

    triu = sb("triu", [128, 128])
    rect = sb("rect", [128, T // 128])
    idn = sb("idn", [128, 128])
    ones = sb("ones", [128, 128])
    sut = sb("sut", [128, 128])
    iotac = sb("iotac", [128, CAP])
    iotab = sb("iotab", [128, 1])
    eoffc = sb("eoffc", [128, E])
    gfm = sb("gfm", [128, KC * E])
    g2fm = sb("g2fm", [128, KC * E])
    avst = sb("avst", [33, 3 * D])
    bvec = avst[0:2, 2 * D:3 * D]
    gbf = sb("gbf", [128, BPC * KC])
    cb1f = sb("cb1f", [128, BPC * KC])
    sbf = sb("sbf", [128, BPC * KC])
    scfm = sb("scfm", [128, KC * BPC])
    abc = sb("abc", [128, D])
    bbc = sb("bbc", [128, D])
    cb1bc = sb("cb1bc", [128, D])
    sbbc = sb("sbbc", [128, D])
    gbbc = sb("gbbc", [128, 2 * D])
    vvec = avst[0:1, 0:D]
    berow = sb("berow", [1, E])
    cmv = avst[0:1, D:2 * D]
    rcmv = sb("rcmv", [1, E])
    dums = sb("dums", [1, 1])
    xt = sb("xt", [128, 2 * D])
    x1t = sb("x1t", [128, 2 * D])
    lnsh = sb("lnsh", [128, D])
    lnab = sb("lnab", [128, D])
    h1m = sb("h1m", [128, D])
    fmt = sb("fmt", [128, D])
    tsum = sb("tsum", [128, D])
    srd = sb("srd", [128, 2 * D])
    x1rb = sb("x1rb", [128, 2 * D])
    wbuf = sb("wbuf", [128, 2 * KC * D])
    gbuf = sb("gbuf", [128, 2 * D])
    gfmb = sb("gfmb", [128, 2 * D])
    ybuf = sb("ybuf", [128, 4 * D])
    bbx = sb("bbx", [128, 2 * D])
    Sb = sb("Sbuf", [128, 2 * CAP])
    rhsE = sb("rhsE", [128, E * 40])
    pgs = sb("pgs", [40, CAP])
    Tc = sb("Tc", [128, 40 * CC])
    zeros8 = sb("zeros8", [128, E])
    gcombs = sb("gcombs", [128, NL])
    invG = sb("invG", [128, NL], I32)
    gidxI = sb("gidxI", [128, 3 * NT * 2], I32)
    gix0 = sb("gix0", [128, 1])
    gix1 = sb("gix1", [128, 1])
    st6 = sb("st6", [128, 12])
    mv1 = sb("mv1", [128, 2])
    vep = sb("vep", [128, 1])
    sq = sb("sq", [128, 1])
    rstd = sb("rstd", [128, 1])
    negm = sb("negm", [128, 1])
    lgt = sb("lgt", [128, E])
    exb = sb("exb", [128, E])
    nmx = sb("nmx", [128, 1])
    ssum = sb("ssum", [128, 1])
    rsum = sb("rsum", [128, 1])
    probs = sb("probs", [128, E])
    pm1 = sb("pm1", [128, 1])
    pm2 = sb("pm2", [128, 1])
    ge1 = sb("ge1", [128, E])
    msk = sb("msk", [128, E])
    combtm = sb("combtm", [128, E])
    selb = sb("selb", [128, E])
    rkb = sb("rkb", [128, E])
    t1b = sb("t1b", [128, E])
    t2b = sb("t2b", [128, E])
    tmp8 = sb("tmp8s", [128, E])
    iotc = sb("iotc", [128, 1])

    RA, RB_ = 3072, 11264

    def modsl(c0, c1):
        if c1 <= 5120:
            return wbuf[0:2, RA + c0: RA + c1]
        return wbuf[0:2, RB_ + (c0 - 5120): RB_ + (c1 - 5120)]

    sa_s, ca_s, ga_s = modsl(0, D), modsl(D, 2 * D), modsl(2 * D, 3 * D)
    sb_s, cb_s, gb_s = modsl(3 * D, 4 * D), modsl(4 * D, 5 * D), modsl(5 * D, 6 * D)
    cs_sl = wbuf[0:2, RB_ + 1024: RB_ + 2048]
    sc_sl = wbuf[0:2, RB_ + 2048: RB_ + 3072]
    gap = cs_sl
    absc = sc_sl
    gbp = sa_s
    avec = ca_s
    cb1v = ga_s
    av0 = avst[32:33, 0:D]
    av1 = avst[32:33, D:2 * D]

    pA = ps("pA", [128, 512])
    pB = ps("pB", [128, 512])
    pC = ps("pC", [128, 512])
    pD = ps("pD", [128, 512])
    pE = ps("pE", [128, 512])
    pF = ps("pF", [128, 512])
    pG = ps("pG", [128, 512])
    pH = ps("pH", [128, 512])

    p = Prog()
    for cn, eng in (("cc", "sync"), ("cx0", "sync"), ("cx1", "sync"),
                    ("cst", "sync"), ("cav", "sync"), ("cbv", "sync"),
                    ("crx0", "sync"), ("crx1", "sync"),
                    ("w0", "scalar"), ("w1", "scalar"),
                    ("c1w0", "scalar"), ("c1w1", "scalar"),
                    ("cxw0", "scalar"), ("cxw1", "scalar"),
                    ("bb0", "scalar"), ("bb1", "scalar"),
                    ("g0", "gpsimd"), ("g1", "gpsimd"),
                    ("gr0", "gpsimd"), ("gr1", "gpsimd"),
                    ("ys0", "sync"), ("ys1", "sync"),
                    ("ys2", "sync"), ("ys3", "sync")):
        p.channel(cn, eng)

    for dst, src in ((triu, trh), (rect, rch), (idn, idh), (ones, onh),
                     (sut, suh), (iotac, ich), (iotab, ibh), (eoffc, eofh),
                     (gfm, gfh)):
        p.dma("cc", lambda e, d=dst, s=src: e.dma_start(out=d[:], in_=s[:]))
    p.dma("cc", lambda e: e.dma_start(out=cs_sl, in_=chh[:]))
    CC_ = p.tok("cc")

    t_silu = p.step("scalar", lambda e: e.activation(sc_sl, cs_sl, AF.Silu), [CC_])
    prev_ev = None
    for k in range(KC):
        tm = p.step("tensor", lambda e, k=k: e.transpose(
            pH[:, 0:2], sc_sl[:, k * 128:(k + 1) * 128], idn[0:2, 0:2]),
            [t_silu, CC_, prev_ev])
        prev_ev = p.step("vector", lambda e, k=k: e.tensor_copy(
            scfm[:, k * BPC:(k + 1) * BPC], pH[:, 0:2]), [tm])
    SCFM = prev_ev

    banks6 = (pA, pB, pC, pD, pE, pF)
    mod_ev = []
    wm_gate = {0: None, 1: None}
    for jh in range(2):
        mm_last = None
        for k in range(KC):
            sl = k % 2
            ld = p.dma(("w0", "w1")[sl], lambda e, jh=jh, k=k, sl=sl: e.dma_start(
                out=wbuf[:, sl * 8192: sl * 8192 + 3072],
                in_=wmh[k * 128:(k + 1) * 128, jh * 3072:(jh + 1) * 3072]),
                [wm_gate[sl]])
            for j6 in range(6):
                mm_last = p.step("tensor", lambda e, k=k, sl=sl, j6=j6: e.matmul(
                    banks6[j6][0:2, :], scfm[:, k * BPC:(k + 1) * BPC],
                    wbuf[:, sl * 8192 + j6 * 512: sl * 8192 + (j6 + 1) * 512],
                    start=(k == 0), stop=(k == KC - 1)),
                    [ld, SCFM] + ([mod_ev[-1]] if (jh == 1 and k == 0 and j6 == 0) else []),
                    noinc=(j6 != 5))
            wm_gate[sl] = p.tok("tensor")
        for j6 in range(6):
            c0_ = jh * 3072 + j6 * 512
            mod_ev.append(p.step("vector", lambda e, c0_=c0_, j6=j6: e.tensor_copy(
                modsl(c0_, c0_ + 512), banks6[j6][0:2, :]), [mm_last]))
    MODDONE = mod_ev[-1]
    wslot_gate = [p.tok("tensor"), p.tok("tensor")]

    t1 = p.step("vector", lambda e: e.tensor_scalar(gap, ga_s, 0.0, None, ALU.is_ge), [MODDONE])
    t2 = p.step("vector", lambda e: e.tensor_scalar(gap, gap, 2.0, -1.0, ALU.mult, ALU.add), [t1])
    t3a = p.step("scalar", lambda e: e.activation(absc, ga_s, AF.Abs), [MODDONE])
    t3 = p.step("vector", lambda e: e.tensor_scalar(absc, absc, 0.25, None, ALU.max), [t3a])
    t4 = p.step("vector", lambda e: e.tensor_tensor(gap, gap, absc, ALU.mult), [t2, t3])
    tB = p.step("vector", lambda e: e.tensor_tensor(bvec[:], sa_s, gap, ALU.mult), [t4, MODDONE])
    t5 = p.step("vector", lambda e: e.tensor_scalar(gbp, gb_s, 0.0, None, ALU.is_ge), [MODDONE, tB])
    t6 = p.step("vector", lambda e: e.tensor_scalar(gbp, gbp, 2.0, -1.0, ALU.mult, ALU.add), [t5])
    t7a = p.step("scalar", lambda e: e.activation(absc, gb_s, AF.Abs), [MODDONE, t4])
    t7 = p.step("vector", lambda e: e.tensor_scalar(absc, absc, 0.25, None, ALU.max), [t7a])
    t8 = p.step("vector", lambda e: e.tensor_tensor(gbp, gbp, absc, ALU.mult), [t6, t7])
    t9 = p.step("vector", lambda e: e.tensor_scalar(avec, ca_s, 1.0, None, ALU.add), [MODDONE])
    tA = p.step("vector", lambda e: e.tensor_tensor(avec, avec, gap, ALU.mult), [t9, t4])
    tC = p.step("vector", lambda e: e.tensor_scalar(cb1v, cb_s, 1.0, None, ALU.add), [MODDONE, t1, t3a])
    tv0 = p.dma("cav", lambda e: e.dma_start(out=av0, in_=avec[0:1, :]), [tA])
    tv1 = p.dma("cav", lambda e: e.dma_start(out=av1, in_=avec[1:2, :]), [tA, tv0])
    tz8 = p.step("vector", lambda e: e.memset(zeros8[:], 0.0), [])
    trE = p.step("vector", lambda e: e.memset(rhsE[:], 0.0), [])
    tr5 = p.step("vector", lambda e: e.memset(rhsE[:, 4:E * 40:45], 1.0), [trE])

    prev_ev = SCFM
    for (srcv, dstf, dep0) in ((cb1v, cb1f, tC), (sb_s, sbf, MODDONE),
                               (gbp, gbf, t8)):
        for k in range(KC):
            tm = p.step("tensor", lambda e, srcv=srcv, k=k: e.transpose(
                pH[:, 4:6], srcv[:, k * 128:(k + 1) * 128], idn[0:2, 0:2]),
                [dep0, prev_ev])
            prev_ev = p.step("vector", lambda e, dstf=dstf, k=k: e.tensor_copy(
                dstf[:, k * BPC:(k + 1) * BPC], pH[:, 4:6]), [tm])
    FMDONE = prev_ev
    ph_a_t = p.tok("tensor")
    ph_a_v = p.tok("vector")

    xt_free = [None, None]
    x1t_free = [None, None]
    misc = {}
    state = {}

    def emit_boundary(b):
        cm0 = p.step("vector", lambda e: e.memset(cmv[0:1, :], 0.0),
                     [misc.get("cmv_r"), misc.get("cmv_w")])
        misc["cmv_w"] = cm0
        prev = None
        for h in range(2):
            bank = (pE, pF)[h]
            tmm = p.step("tensor", lambda e, b=b, h=h, bank=bank: e.matmul(
                bank[:, 0:512], ones[32:33, :],
                (av0 if b == 0 else av1)[:, h * 512:(h + 1) * 512],
                start=True, stop=True),
                [tv1, CC_, FMDONE, misc.get("carry_upd"), misc.get("rk_r"),
                 prev])
            prev = p.step("vector", lambda e, h=h, bank=bank: e.tensor_copy(
                abc[:, h * 512:(h + 1) * 512], bank[:, 0:512]), [tmm])
        tvb = p.dma("cbv", lambda e, b=b: e.dma_start(
            out=vvec[0:1, :], in_=bvec[b:b + 1, :]),
            [tB, misc.get("vvec_r"), prev])
        for h in range(2):
            bank = (pE, pF)[h]
            tmm = p.step("tensor", lambda e, h=h, bank=bank: e.matmul(
                bank[:, 0:512], ones[0:1, :], vvec[0:1, h * 512:(h + 1) * 512],
                start=True, stop=True), [tvb, prev])
            prev = p.step("vector", lambda e, h=h, bank=bank: e.tensor_copy(
                bbc[:, h * 512:(h + 1) * 512], bank[:, 0:512]), [tmm])
        for (vecf, base) in ((cb1f, cb1bc), (sbf, sbbc), (gbf, None)):
            vts = []
            for k in range(KC):
                bank = (pE, pF)[k // 4]
                off = (k % 4) * 128
                vts.append(p.step("tensor", lambda e, vecf=vecf, k=k, b=b, bank=bank, off=off: e.transpose(
                    bank[0:1, off:off + 128],
                    vecf[:, k * BPC + b: k * BPC + b + 1], idn[:]),
                    [FMDONE, prev]))
            e0 = p.step("vector", lambda e: e.tensor_copy(
                vvec[0:1, 0:512], pE[0:1, 0:512]), [vts[3], prev])
            e1 = p.step("vector", lambda e: e.tensor_copy(
                vvec[0:1, 512:1024], pF[0:1, 0:512]), [vts[7], e0])
            for h in range(2):
                bank = (pE, pF)[h]
                if base is None:
                    dst = gbbc[:, b * D + h * 512: b * D + (h + 1) * 512]
                else:
                    dst = base[:, h * 512:(h + 1) * 512]
                tmm = p.step("tensor", lambda e, h=h, bank=bank: e.matmul(
                    bank[:, 0:512], ones[0:1, :],
                    vvec[0:1, h * 512:(h + 1) * 512],
                    start=True, stop=True), [e0, e1])
                prev = p.step("vector", lambda e, bank=bank, dst=dst: e.tensor_copy(
                    dst, bank[:, 0:512]), [tmm])
        misc["vvec_r"] = prev
        g2t = None
        for k in range(KC):
            g2t = p.step("vector", lambda e, k=k, b=b: e.tensor_scalar(
                g2fm[:, k * E:(k + 1) * E], gfm[:, k * E:(k + 1) * E],
                cb1f[:, k * BPC + b: k * BPC + b + 1], None, ALU.mult),
                [CC_, FMDONE, misc.get("g2_r"), g2t])
        misc["G2"] = g2t
        bem = None
        for k in range(KC):
            bem = p.step("tensor", lambda e, k=k, b=b: e.matmul(
                pF[0:1, 0:E], sbf[:, k * BPC + b: k * BPC + b + 1],
                gfm[:, k * E:(k + 1) * E],
                start=(k == 0), stop=(k == KC - 1)),
                [FMDONE, CC_, prev] if k == 0 else ([FMDONE] if k == KC - 1 else []),
                noinc=(k != KC - 1))
        misc["BER"] = p.step("vector", lambda e: e.tensor_copy(
            berow[0:1, :], pF[0:1, 0:E]), [bem])
        misc["ABC"] = misc["BER"]

    def ln_chain(src_ap, deps_in):
        tst0 = p.step("vector", lambda e: e.bn_stats(
            st6[:, 0:6], src_ap[:, 0:512]), deps_in + [misc.get("aggr")])
        tst = p.step("vector", lambda e: e.bn_stats(
            st6[:, 6:12], src_ap[:, 512:1024]), deps_in + [tst0])
        tag = p.step("vector", lambda e: e.bn_aggr(mv1[:], st6[:]),
                     [tst0, tst, misc.get("vep"), misc.get("ln_s")])
        misc["aggr"] = tag
        tve = p.step("vector", lambda e: e.tensor_scalar(
            vep[:], mv1[:, 1:2], EPS, None, ALU.add), [tag, misc.get("sq_s")])
        misc["vep"] = tve
        tsq = p.step("scalar", lambda e: e.activation(sq[:], vep[:], AF.Sqrt),
                     [tve, misc.get("rstd_w")])
        misc["sq_s"] = tsq
        trs = p.step("vector", lambda e: e.reciprocal(rstd[:], sq[:]),
                     [tsq, misc.get("ln_s")])
        misc["rstd_w"] = trs
        app = p.step("vector", lambda e: e.tensor_scalar(
            lnsh[:], src_ap, mv1[:, 0:1], rstd[:], ALU.subtract, ALU.mult),
            deps_in + [trs, tag, misc.get("lnsh_r"), misc.get("lnsh_r2")])
        misc["ln_s"] = app
        return app

    def emit_xld(q, i):
        s = i % 2
        state[("xld", q, i)] = p.dma(("cx0", "cx1")[s], lambda e, q=q, i=i, s=s: e.dma_start(
            out=xt[:, s * D:(s + 1) * D],
            in_=xh[q // QPB, ((q % QPB) * NT + i) * 128:((q % QPB) * NT + i + 1) * 128, :]),
            [xt_free[s]])

    def tile_gen(q, i):
        b = q // QPB
        par = q % 2
        gq = (q % QPB) * NT + i
        s = i % 2
        xp = i % 2
        if i == 0 and q % QPB == 0:
            emit_boundary(b)
        if i == 0:
            rc0 = p.step("vector", lambda e: e.memset(rcmv[0:1, :], 0.0),
                         [misc.get("rk_r"), misc.get("rk_w")])
            misc["rk_w"] = rc0
        ld = state[("xld", q, i)]
        tln = ln_chain(xt[:, s * D:(s + 1) * D], [ld])
        tlnA = p.step("vector", lambda e: e.tensor_tensor(
            lnab[:], lnsh[:], abc[:], ALU.mult),
            [tln, misc.get("ABC"), misc.get("cum_mm")])
        misc["lnsh_r"] = tlnA
        yield
        cms = []
        for h in range(2):
            bank = (pE, pH)[h]
            p.step("tensor", lambda e, h=h, bank=bank: e.matmul(
                bank[:, 0:512], triu[:],
                lnab[:, h * 512:(h + 1) * 512], start=True, stop=False),
                [tlnA, CC_, misc.get("x1_add"),
                 (misc.get("lnsh_r") if h == 1 else None)], noinc=True)
            cm = p.step("tensor", lambda e, h=h, bank=bank: e.matmul(
                bank[:, 0:512], ones[0:1, :],
                cmv[0:1, h * 512:(h + 1) * 512], start=False, stop=True),
                [misc.get("cmv_w")])
            cms.append(cm)
            misc["cum_mm"] = cm
        tot0 = p.step("tensor", lambda e: e.matmul(
            pF[0:1, 0:512], ones[:, 0:1], lnab[:, 0:512],
            start=True, stop=True),
            [tlnA, CC_, misc.get("carry_upd"), misc.get("rk_r")])
        misc["cmv_r"] = cms[1]
        yield
        xf = x1t_free[xp]
        adds = []
        for h in range(2):
            bank = (pE, pH)[h]
            a1 = p.step("vector", lambda e, gq=gq, h=h, s=s, xp=xp, bank=bank: e.scalar_tensor_tensor(
                x1t[:, xp * D + h * 512: xp * D + (h + 1) * 512],
                bank[:, 0:512], rect[:, gq:gq + 1],
                xt[:, s * D + h * 512: s * D + (h + 1) * 512], ALU.mult, ALU.add),
                [cms[h], ld, (xf[0] if xf else None), CC_])
            if h == 0 and xf:
                state[("x1cov",) + xf[1]] = a1
            a2 = p.step("vector", lambda e, h=h, xp=xp: e.tensor_tensor(
                x1t[:, xp * D + h * 512: xp * D + (h + 1) * 512],
                x1t[:, xp * D + h * 512: xp * D + (h + 1) * 512],
                bbc[:, h * 512:(h + 1) * 512], ALU.add),
                [a1, misc.get("ABC")])
            adds.append(a2)
            misc["x1_add"] = a2
            if h == 0:
                cu0 = p.step("vector", lambda e: e.tensor_tensor(
                    cmv[0:1, 0:512], cmv[0:1, 0:512],
                    pF[0:1, 0:512], ALU.add), [tot0, cms[0], misc.get("cmv_w")])
                misc["cmv_w"] = cu0
        cxw = ("cxw0", "cxw1")[xp]
        x1w = p.dma(cxw, lambda e, i=i, q=q, xp=xp: e.dma_start(
            out=x1sv[q % 3][i * 128:(i + 1) * 128, :], in_=x1t[:, xp * D:(xp + 1) * D]),
            [adds[1], state.get(("rbx", q % 3))])
        x1t_free[xp] = (x1w, (q, i))
        xt_free[s] = adds[1]
        tot1 = p.step("tensor", lambda e: e.matmul(
            pF[0:1, 0:512], ones[:, 0:1], lnab[:, 512:1024],
            start=True, stop=True),
            [tlnA, CC_, cu0, misc.get("rk_r")])
        if i + 1 < NT:
            emit_xld(q, i + 1)
        yield
        cu1 = p.step("vector", lambda e: e.tensor_tensor(
            cmv[0:1, 512:1024], cmv[0:1, 512:1024],
            pF[0:1, 0:512], ALU.add), [tot1, cms[1], misc.get("cmv_w")])
        misc["carry_upd"] = cu1
        misc["cmv_w"] = cu1
        th1 = ln_chain(x1t[:, xp * D:(xp + 1) * D], [adds[0], adds[1]])
        m1t = p.step("vector", lambda e: e.tensor_tensor(
            h1m[:], lnsh[:], cb1bc[:], ALU.mult),
            [th1, misc.get("vvec_r"), misc.get("h1m_w")])
        m2t = p.step("vector", lambda e: e.tensor_tensor(
            h1m[:], h1m[:], sbbc[:], ALU.add), [m1t])
        misc["lnsh_r2"] = m2t
        misc["h1m_w"] = p.dma(("c1w0", "c1w1")[par], lambda e, i=i, par=par: e.dma_start(
            out=h1sv[par][i * 128:(i + 1) * 128, :], in_=h1m[:]),
            [m2t] + state.get(("gath", par), []))
        fme = []
        for k in range(KC):
            bank = (pH, pE)[k % 2]
            off = (k // 2) * 128
            tmk = p.step("tensor", lambda e, k=k, bank=bank, off=off: e.transpose(
                bank[:, off:off + 128], lnsh[:, k * 128:(k + 1) * 128], idn[:]),
                [th1, misc.get(("fme", k % 2)),
                 (misc.get("x1_add") if k < 2 else None)])
            fme.append(p.step("vector", lambda e, k=k, bank=bank, off=off: e.tensor_copy(
                fmt[:, k * 128:(k + 1) * 128], bank[:, off:off + 128]),
                [tmk, misc.get("lg_mm")]))
            misc[("fme", k % 2)] = fme[-1]
        misc["lnsh_r"] = p.tok("tensor")
        for k in range(KC):
            p.step("tensor", lambda e, k=k: e.matmul(
                pF[:, 0:E], fmt[:, k * 128:(k + 1) * 128],
                g2fm[:, k * E:(k + 1) * E], start=(k == 0), stop=False),
                [fme[k], misc.get("G2")] + ([misc.get("lg_ev"), misc.get("carry_upd"),
                                             misc.get("rk_r")] if k == 0 else []),
                noinc=True)
        lgm = p.step("tensor", lambda e: e.matmul(
            pF[:, 0:E], ones[0:1, :], berow[0:1, :], start=False, stop=True),
            [misc.get("BER"), fme[-1]])
        misc["lg_mm"] = lgm
        lge = p.step("vector", lambda e: e.tensor_copy(lgt[:], pF[:, 0:E]),
                     [lgm, misc.get("smx")])
        misc["lg_ev"] = lge
        trm = p.step("vector", lambda e: e.reduce_max(pm1[:], lgt[:], axis=AX.X),
                     [lge, misc.get("t18")])
        tnr = p.step("vector", lambda e: e.tensor_scalar(
            nmx[:], pm1[:], -1.0, None, ALU.mult), [trm, misc.get("exp_s")])
        tex = p.step("scalar", lambda e: e.activation(
            exb[:], lgt[:], AF.Exp, bias=nmx[:], scale=f32(1.0)),
            [lge, tnr, misc.get("probs_r")])
        misc["exp_s"] = tex
        tsu = p.step("vector", lambda e: e.reduce_sum(ssum[:], exb[:], axis=AX.X),
                     [tex])
        trc = p.step("vector", lambda e: e.reciprocal(rsum[:], ssum[:]), [tsu])
        tpr = p.step("vector", lambda e: e.tensor_scalar(
            probs[:], exb[:], rsum[:], None, ALU.mult),
            [trc, tex, misc.get("probs_r")])
        tm1 = p.step("vector", lambda e: e.reduce_max(pm1[:], probs[:], axis=AX.X),
                     [tpr, trm, tnr])
        tg1 = p.step("vector", lambda e: e.tensor_scalar(
            ge1[:], probs[:], pm1[:], None, ALU.is_ge), [tm1, tpr])
        tms = p.step("vector", lambda e: e.scalar_tensor_tensor(
            msk[:], ge1[:], -1e30, probs[:], ALU.mult, ALU.add), [tg1, tpr])
        tm2 = p.step("vector", lambda e: e.reduce_max(pm2[:], msk[:], axis=AX.X),
                     [tms])
        tg2 = p.step("vector", lambda e: e.tensor_scalar(
            ge1[:], probs[:], pm2[:], None, ALU.is_ge), [tm2, tpr, tms, tg1])
        tcb = p.step("vector", lambda e: e.tensor_tensor(
            combtm[:], ge1[:], probs[:], ALU.mult), [tg2, tpr])
        misc["smx"] = tcb
        tsl = p.step("vector", lambda e: e.tensor_scalar(
            selb[:], combtm[:], 0.0, None, ALU.is_gt), [tcb, misc.get("rk_mm")])
        t18 = p.step("vector", lambda e: e.tensor_scalar(
            t1b[:], probs[:], pm1[:], None, ALU.is_ge), [tpr, tm1])
        misc["t18"] = t18
        misc["probs_r"] = t18
        yield
        rk1 = p.step("tensor", lambda e: e.matmul(
            pF[:, 32:40], sut[:], selb[:], start=True, stop=False),
            [tsl, CC_, misc.get("rkb_w"), lgm], noinc=True)
        rk2 = p.step("tensor", lambda e: e.matmul(
            pF[:, 32:40], ones[0:1, :], rcmv[0:1, :], start=False, stop=True),
            [misc.get("rk_w"), tsl])
        rk3 = p.step("tensor", lambda e: e.matmul(
            pF[0:1, 48:56], ones[:, 0:1], selb[:], start=True, stop=True),
            [tsl, misc.get("rk_r")])
        misc["rk_mm"] = rk3
        trk = p.step("vector", lambda e: e.tensor_scalar(
            rkb[:], pF[:, 32:40], 1.0, None, ALU.add), [rk2, rk3, misc.get("S_w")])
        trk2 = p.step("vector", lambda e: e.tensor_tensor(
            rkb[:], rkb[:], selb[:], ALU.mult), [trk, tsl])
        trk3 = p.step("vector", lambda e: e.tensor_scalar(
            rkb[:], rkb[:], -1.0, None, ALU.add), [trk2])
        misc["rkb_w"] = trk3
        rcu = p.step("vector", lambda e: e.tensor_tensor(
            rcmv[0:1, :], rcmv[0:1, :], pF[0:1, 48:56], ALU.add),
            [rk3, rk2, misc.get("rk_w")])
        misc["rk_w"] = rcu
        misc["rk_r"] = rcu
        tic = p.step("vector", lambda e, i=i: e.tensor_scalar(
            iotc[:], iotab[:], float(i * 128), None, ALU.add), [CC_])
        t28 = p.step("vector", lambda e: e.tensor_tensor(
            t2b[:], selb[:], t1b[:], ALU.subtract), [t18, tsl])
        tt8 = p.step("vector", lambda e: e.tensor_tensor(
            tmp8[:], eoffc[:], rkb[:], ALU.add), [trk3, CC_])
        qs3 = (q % 3) * NT * 2
        for (selm, gx, chv) in ((t1b, gix0, 0), (t2b, gix1, 1)):
            gm = p.step("vector", lambda e, selm=selm: e.tensor_tensor(
                msk[:], selm[:], tmp8[:], ALU.mult), [tt8, t28])
            gr = p.step("vector", lambda e, gx=gx: e.reduce_sum(
                gx[:], msk[:], axis=AX.X), [gm])
            gc = p.step("vector", lambda e, gx=gx: e.tensor_scalar(
                gx[:], gx[:], float(E * CAP - 1), None, ALU.min), [gr])
            p.step("vector", lambda e, gx=gx, col=qs3 + i * 2 + chv: e.tensor_copy(
                gidxI[:, col:col + 1], gx[:]), [gc])
        ra0 = p.step("vector", lambda e: e.tensor_scalar(
            rhsE[:, 0:E * 40:45], zeros8[:], iotc[:], None, ALU.add),
            [tic, tz8, trE, misc.get("rhs_mm")])
        ra1 = p.step("vector", lambda e: e.tensor_copy(
            rhsE[:, 1:E * 40:45], combtm[:]), [tcb, ra0])
        misc["RHS"] = ra1
        yield
        RHS = misc["RHS"]
        for ei in range(E):
            ssl = (ei % 2) * CAP
            tS = p.step("vector", lambda e, ei=ei, ssl=ssl: e.tensor_scalar(
                Sb[:, ssl:ssl + CAP], iotac[:], rkb[:, ei:ei + 1], None,
                ALU.is_equal),
                [trk3, CC_, misc.get(("S_mm", ei % 2))])
            misc["S_w"] = tS
            first = (i == 0 and ei == 0)
            last = (i == NT - 1 and ei == E - 1)
            p.step("tensor", lambda e, ei=ei, ssl=ssl, first=first, last=last: e.matmul(
                pG[0:40, 0:CAP],
                rhsE[:, ei * 40:(ei + 1) * 40],
                Sb[:, ssl:ssl + CAP],
                start=first, stop=last),
                [tS, RHS, tr5] + ([misc.get("pG_free")] if first else []))
            misc[("S_mm", ei % 2)] = p.tok("tensor")
        misc["rhs_mm"] = p.tok("tensor")

    def emit_idx(q):
        SLOTS_MM = p.tok("tensor")
        SCAL = p.tok("scalar")
        gpg = [Tok("g0", p.count["g0"]), Tok("g1", p.count["g1"])]
        evac = p.step("vector", lambda e: e.tensor_copy(
            pgs[:], pG[0:40, 0:CAP]), [SLOTS_MM, misc.get("pgs_r")])
        tcs = []
        for cc in range(CC):
            bank = (pG, pH)[cc % 2]
            tt = p.step("tensor", lambda e, cc=cc, bank=bank: e.transpose(
                bank[:, 0:40],
                pgs[0:40, cc * 128:(cc + 1) * 128], idn[0:40, 0:40]),
                [evac, (tcs[cc - 2] if cc >= 2 else None)])
            tcs.append(p.step("vector", lambda e, cc=cc, bank=bank: e.tensor_copy(
                Tc[:, cc * 40:(cc + 1) * 40], bank[:, 0:40]),
                [tt]))
        misc["pgs_r"] = p.tok("tensor")
        ev1 = ev2 = None
        for cc in range(CC):
            ev1 = p.step("vector", lambda e, cc=cc: e.tensor_copy(
                gcombs[:, cc:NL:CC], Tc[:, cc * 40 + 1: (cc + 1) * 40: 5]),
                [tcs[cc], SCAL, ev1])
            ev2 = p.step("vector", lambda e, cc=cc: e.tensor_copy(
                invG[:, cc:NL:CC], Tc[:, cc * 40: (cc + 1) * 40: 5]),
                [tcs[cc]] + (gpg if cc == 0 else [ev2]))
        state[("IDX", q)] = p.tok("vector")
        state[("ev1", q)] = ev1
        misc["pG_free"] = state[("IDX", q)]

    def emit_gather(q, j, blocks):
        par = q % 2
        ei, cc = blocks[j]
        gp_ = j % 2
        gsl = gp_ * D
        state[("gath_t", q, j)] = (p.dma(("g0", "g1")[gp_], lambda e, ei=ei, cc=cc, gsl=gsl, par=par: e.indirect_dma_start(
            out=gbuf[:, gsl:gsl + D], out_offset=None,
            in_=h1sv[par][:, :],
            in_offset=IndirectOffsetOnAxis(invG[:, ei * CC + cc: ei * CC + cc + 1], axis=0)),
            [state[("H1W", q)], state[("IDX", q)], misc.get(("gb_free", gp_))]), gsl)

    def emit_tsc(q, j, blocks):
        ei, cc = blocks[j]
        gt, gsl = state[("gath_t", q, j)]
        state[("tsc", q, j)] = p.step("scalar", lambda e, ei=ei, cc=cc, gsl=gsl: e.activation(
            gbuf[:, gsl:gsl + D], gbuf[:, gsl:gsl + D], AF.Identity,
            bias=f32(0.0), scale=gcombs[:, ei * CC + cc: ei * CC + cc + 1]),
            [gt, state[("ev1", q)]])

    def emit_wload(ei):
        wsl = ei % 2
        wl = None
        extra = misc.pop("wsg0", [])
        for lh in range(2):
            wl = p.dma(("w0", "w1")[wsl], lambda e, ei=ei, wsl=wsl, lh=lh: e.dma_start(
                out=wbuf[:, wsl * 8192 + lh * 4096: wsl * 8192 + (lh + 1) * 4096]
                    .rearrange("p (k o) -> p k o", o=D),
                in_=wth[ei, lh * 512:(lh + 1) * 512, :]
                    .rearrange("(k p) o -> p k o", p=128)),
                ([wslot_gate[wsl]] + extra) if lh == 0 else [])
        misc[("wl", wsl)] = wl
        misc[("bbl_t", wsl)] = p.dma(("bb0", "bb1")[wsl], lambda e, ei=ei, wsl=wsl: e.dma_start(
            out=bbx[:, wsl * D:(wsl + 1) * D], in_=bbeh[ei]),
            [misc.get(("bbx_free", wsl))])

    def emit_moe_block(q, j, blocks):
        ei, cc = blocks[j]
        wl = misc[("wl", ei % 2)]
        gt, gsl = state[("gath_t", q, j)]
        gp_ = j % 2
        tsc = state.pop(("tsc", q, j))
        tps = []
        for k in range(KC):
            bank = (pC, pD)[k % 2]
            off = (k // 2) * 128
            tmk = p.step("tensor", lambda e, k=k, gsl=gsl, bank=bank, off=off: e.transpose(
                bank[:, off:off + 128], gbuf[:, gsl + k * 128: gsl + (k + 1) * 128],
                idn[:]),
                [tsc, misc.get(("gf_ev", k % 2))])
            tps.append(p.step("vector", lambda e, k=k, gsl=gsl, bank=bank, off=off: e.tensor_copy(
                gfmb[:, gsl + k * 128: gsl + (k + 1) * 128], bank[:, off:off + 128]),
                [tmk, misc.get("moe_mm")]))
            misc[("gf_ev", k % 2)] = tps[-1]
        misc[("gb_free", gp_)] = p.tok("tensor")
        if j + 2 < len(blocks):
            emit_gather(q, j + 2, blocks)
        if j + 1 < len(blocks):
            emit_tsc(q, j + 1, blocks)
        for og in range(2):
            bank = (pA, pB)[og]
            mmt = None
            for k in range(KC):
                mmt = p.step("tensor", lambda e, k=k, gsl=gsl, wsl=ei % 2, og=og, bank=bank: e.matmul(
                    bank[:, 0:512],
                    gfmb[:, gsl + k * 128: gsl + (k + 1) * 128],
                    wbuf[:, wsl * 8192 + k * D + og * 512: wsl * 8192 + k * D + og * 512 + 512],
                    start=(k == 0), stop=(k == KC - 1)),
                    [tps[k]] + ([wl, misc.get(("pm_free", og))] if k == 0 else []),
                    noinc=(k != KC - 1))
            yev = p.step("vector", lambda e, j=j, og=og, bank=bank, slE=ei % 2, col=ei * CC + cc: e.scalar_tensor_tensor(
                ybuf[:, (j % 4) * D + og * 512: (j % 4) * D + (og + 1) * 512],
                bbx[:, slE * D + og * 512: slE * D + (og + 1) * 512],
                gcombs[:, col:col + 1],
                bank[:, 0:512], ALU.mult, ALU.add),
                [mmt, misc.get(("bbl_t", ei % 2)),
                 Tok("ys%d" % (j % 4), p.count["ys%d" % (j % 4)]),
                 misc.get("moe_mm")])
            misc["moe_mm"] = yev
            misc[("pm_free", og)] = yev
        par = q % 2
        p.dma("ys%d" % (j % 4), lambda e, ei=ei, cc=cc, j=j, par=par: e.dma_start(
            out=yslabv[par][(ei * CC + cc) * 128:(ei * CC + cc + 1) * 128, :],
            in_=ybuf[:, (j % 4) * D:(j % 4 + 1) * D]),
            [misc["moe_mm"], state.get(("ysl_free", par))])
        if cc == CC - 1:
            wslot_gate[ei % 2] = p.tok("tensor")
            misc[("bbx_free", ei % 2)] = misc["moe_mm"]
            if ei + 2 < E:
                emit_wload(ei + 2)
            elif q + 1 < NQ:
                emit_wload(ei + 2 - E)

    def rb_load01(qr, s2):
        par = qr % 2
        qs3 = (qr % 3) * NT * 2
        state[("rb_r0", s2)] = p.dma("gr0", lambda e, i=s2, par=par, qs3=qs3: e.indirect_dma_start(
            out=srd[:, 0:D], out_offset=None,
            in_=yslabv[par][:, :],
            in_offset=IndirectOffsetOnAxis(gidxI[:, qs3 + i * 2: qs3 + i * 2 + 1], axis=0)),
            [state.get("SFLt"), misc.get("rb_ts1")])
        state[("rb_r1", s2)] = p.dma("gr1", lambda e, i=s2, par=par, qs3=qs3: e.indirect_dma_start(
            out=srd[:, D:2 * D], out_offset=None,
            in_=yslabv[par][:, :],
            in_offset=IndirectOffsetOnAxis(gidxI[:, qs3 + i * 2 + 1: qs3 + i * 2 + 2], axis=0)),
            [state.get("SFLt"), misc.get("rb_ts1")])

    def rb_loadx(qr, s2, tail):
        if tail:
            cov = [Tok("cxw0", p.count["cxw0"]), Tok("cxw1", p.count["cxw1"])]
        else:
            cov = [state.get(("x1cov", qr, s2))]
        state[("rb_rx", s2)] = p.dma(("crx0", "crx1")[s2 % 2],
                                     lambda e, i=s2, qr=qr: e.dma_start(
            out=x1rb[:, (i % 2) * D:(i % 2 + 1) * D],
            in_=x1sv[qr % 3][i * 128:(i + 1) * 128, :]),
            cov + [misc.get("rb_to")])

    def rb_slot(qr, s, tail=False):
        b = qr // QPB
        ts1 = p.step("vector", lambda e: e.tensor_tensor(
            tsum[:], srd[:, 0:D], srd[:, D:2 * D], ALU.add),
            [state[("rb_r0", s)], state[("rb_r1", s)],
             Tok("cst", p.count["cst"])])
        misc["rb_ts1"] = ts1
        if s + 1 < NT:
            rb_load01(qr, s + 1)
        to = None
        for og in range(2):
            tg = p.step("vector", lambda e, og=og, b=b: e.tensor_tensor(
                tsum[:, og * 512:(og + 1) * 512], tsum[:, og * 512:(og + 1) * 512],
                gbbc[:, b * D + og * 512: b * D + (og + 1) * 512], ALU.mult),
                [ts1, misc.get("vvec_r")])
            to = p.step("vector", lambda e, og=og, s=s: e.tensor_tensor(
                tsum[:, og * 512:(og + 1) * 512], tsum[:, og * 512:(og + 1) * 512],
                x1rb[:, (s % 2) * D + og * 512: (s % 2) * D + og * 512 + 512],
                ALU.add), [tg, state[("rb_rx", s)]])
        misc["rb_to"] = to
        gq = (qr % QPB) * NT + s
        p.dma("cst", lambda e, b=b, gq=gq: e.dma_start(
            out=out[b, gq * 128:(gq + 1) * 128, :], in_=tsum[:]), [to])
        if s + 2 < NT:
            rb_loadx(qr, s + 2, tail)
        if s == NT - 1:
            state[("ysl_free", qr % 2)] = ts1
            state[("rbx", qr % 3)] = to

    def rb_prime(qr, tail=False):
        rb_load01(qr, 0)
        rb_loadx(qr, 0, tail)
        if NT > 1:
            rb_loadx(qr, 1, tail)

    blocks = [(ei, cc) for ei in range(E) for cc in range(CC)]
    misc["wsg0"] = [FMDONE, tv1, ph_a_t, ph_a_v]
    emit_xld(0, 0)
    for i in range(NT):
        for _ in tile_gen(0, i):
            pass
    emit_idx(0)
    state[("H1W", 0)] = Tok("c1w0", p.count["c1w0"])
    for q in range(NQ):
        par = q % 2
        if q == 0:
            emit_wload(0)
            emit_wload(1)
        emit_gather(q, 0, blocks)
        emit_gather(q, 1, blocks)
        emit_tsc(q, 0, blocks)
        tgens = {}
        if q + 1 < NQ:
            emit_xld(q + 1, 0)
            tgens[0] = tile_gen(q + 1, 0)
        if q >= 1:
            rb_prime(q - 1)
        if 0 in tgens:
            next(tgens[0])
        for j in range(NB):
            ti, ph = j // 3, j % 3
            g = tgens.get(ti)
            if ph == 0 and g is not None:
                next(g, None)
            emit_moe_block(q, j, blocks)
            if ph == 0 and g is not None:
                next(g, None)
            elif ph == 1:
                if g is not None:
                    next(g, None)
            elif ph == 2:
                if g is not None:
                    next(g, None)
                    next(g, None)
                    del tgens[ti]
                if q + 1 < NQ and ti + 1 < NT:
                    tgens[ti + 1] = tile_gen(q + 1, ti + 1)
                    next(tgens[ti + 1])
                if q >= 1 and ti < NT:
                    rb_slot(q - 1, ti)
        state[("gath", par)] = [Tok("g0", p.count["g0"]), Tok("g1", p.count["g1"])]
        state["SFLt"] = p.step("vector", lambda e: e.memset(dums[:], 0.0),
                               [Tok("ys%d" % k_, p.count["ys%d" % k_])
                                for k_ in range(4)])
        if q + 1 < NQ:
            emit_idx(q + 1)
            cql = ("c1w0", "c1w1")[(q + 1) % 2]
            state[("H1W", q + 1)] = Tok(cql, p.count[cql])
        misc["g2_r"] = p.tok("tensor")
    rb_prime(NQ - 1, tail=True)
    for s in range(NT):
        rb_slot(NQ - 1, s, tail=True)

    p.wait("sync", [p.tok("cst")])
    p.wait("gpsimd", [p.tok("cst")])

    p.alloc_sems(es, nc)
    with es:
        with nc.Block() as block:
            p.replay(block)
    return nc




def make_consts(T, QT):
    TRIU = np.triu(np.ones((128, 128), dtype=np.float32))
    RECT = np.zeros((128, T // 128), dtype=np.float32)
    for i in range(T // 128):
        RECT[:, i] = (1.0 / (i * 128 + np.arange(128) + 1.0)).astype(np.float32)
    IDN = np.eye(128, dtype=np.float32)
    ONES = np.ones((128, 128), dtype=np.float32)
    SUT = np.triu(np.ones((128, 128), dtype=np.float32), 1)
    IOTAC = np.broadcast_to(np.arange(CAP, dtype=np.float32), (128, CAP)).copy()
    IOTB = np.arange(128, dtype=np.float32).reshape(128, 1).copy()
    EOFF = np.broadcast_to(CAP * np.arange(E, dtype=np.float32), (128, E)).copy()
    return TRIU, RECT, IDN, ONES, SUT, IOTAC, IOTB, EOFF


def make_in_maps(x, gate, ew, eb, c, wmod, T, QT, n_cores=8):
    WT = np.ascontiguousarray(ew.transpose(0, 2, 1))
    gfm = np.ascontiguousarray(
        gate.reshape(E, KC, 128).transpose(2, 1, 0).reshape(128, KC * E))
    BBE = np.ascontiguousarray(
        np.broadcast_to(eb[:, None, :], (E, 128, eb.shape[1])))
    TRIU, RECT, IDN, ONES, SUT, IOTAC, IOTB, EOFF = make_consts(T, QT)
    maps = []
    for k in range(n_cores):
        maps.append({
            "x": np.ascontiguousarray(x[BPC * k:BPC * (k + 1)]),
            "c": np.ascontiguousarray(c[BPC * k:BPC * (k + 1)]),
            "gfmh": gfm, "WT": WT, "BBEh": BBE,
            "wmod": np.ascontiguousarray(wmod),
            "TRIUh": TRIU, "RECTh": RECT, "IDNh": IDN, "ONESh": ONES,
            "SUTh": SUT, "IOTACh": IOTAC, "IOTBh": IOTB, "EOFFh": EOFF,
        })
    return maps


def kernel(L0_attn_x, L0_ffn_gate_w, L0_ffn_expert_w, L0_ffn_expert_b, L0_c,
           L0_wmod):
    x = np.ascontiguousarray(np.asarray(L0_attn_x, dtype=np.float32))
    gate = np.asarray(L0_ffn_gate_w, dtype=np.float32)
    ew = np.asarray(L0_ffn_expert_w, dtype=np.float32)
    eb = np.asarray(L0_ffn_expert_b, dtype=np.float32)
    c = np.asarray(L0_c, dtype=np.float32)
    wmod = np.asarray(L0_wmod, dtype=np.float32)
    T = x.shape[1]
    QT = 1024
    nc = build_nc(T=T, QT=QT)
    maps = make_in_maps(x, gate, ew, eb, c, wmod, T, QT)
    res = run_bass_kernel_spmd(nc, maps, core_ids=list(range(8)))
    return np.concatenate([np.asarray(r["out"]) for r in res.results], axis=0)


# version 23
# speedup vs baseline: 1.0250x; 1.0250x over previous
import numpy as np
import concourse.bass as bass
import concourse.mybir as mybir
from concourse.bass import IndirectOffsetOnAxis
from concourse.bass_utils import run_bass_kernel_spmd

F32 = mybir.dt.float32
I32 = mybir.dt.int32
AF = mybir.ActivationFunctionType
ALU = mybir.AluOpType
AX = mybir.AxisListType
f32 = np.float32



class Tok:
    __slots__ = ("key", "val")

    def __init__(self, key, val):
        self.key = key
        self.val = val


class Step:
    __slots__ = ("eng", "key", "fn", "deps", "inc", "val")

    def __init__(self, eng, key, fn, deps, inc, val):
        self.eng = eng
        self.key = key
        self.fn = fn
        self.deps = deps
        self.inc = inc
        self.val = val


class Prog:

    COMPUTE = ("sync", "vector", "scalar", "tensor", "gpsimd")

    def __init__(self):
        self.steps = []
        self.count = {e: 0 for e in self.COMPUTE}
        self.channels = {}

    def channel(self, name, eng):
        self.channels[name] = eng
        self.count[name] = 0
        return name

    def step(self, eng, fn, deps=(), noinc=False):
        inc = 0 if noinc else 1
        self.count[eng] += inc
        s = Step(eng, eng, fn, [d for d in deps if d is not None], inc,
                 self.count[eng])
        self.steps.append(s)
        return Tok(eng, s.val)

    def dma(self, chan, fn, deps=()):
        eng = self.channels[chan]
        self.count[chan] += 16
        s = Step(eng, chan, fn, [d for d in deps if d is not None], 16,
                 self.count[chan])
        self.steps.append(s)
        return Tok(chan, s.val)

    def wait(self, eng, deps):
        s = Step(eng, eng, None, [d for d in deps if d is not None], 0,
                 self.count[eng])
        self.steps.append(s)
        return Tok(eng, self.count[eng])

    def tok(self, key):
        return Tok(key, self.count[key])

    def alloc_sems(self, es, nc):
        self.sems = {}
        for e in self.COMPUTE:
            self.sems[e] = es.enter_context(nc.semaphore(f"ch_{e}"))
        for c in self.channels:
            self.sems[c] = es.enter_context(nc.semaphore(f"ch_{c}"))

    def replay(self, block):
        sems = self.sems

        def body(engname):
            def run(engobj):
                hwm = {}
                for s in self.steps:
                    if s.eng != engname:
                        continue
                    best = {}
                    for d in s.deps:
                        if d.val > best.get(d.key, 0):
                            best[d.key] = d.val
                    for k2, v2 in best.items():
                        if v2 > hwm.get(k2, 0):
                            engobj.wait_ge(sems[k2], v2)
                            hwm[k2] = v2
                    if s.fn is None:
                        continue
                    inst = s.fn(engobj)
                    if s.inc:
                        inst.then_inc(sems[s.key], s.inc)
            return run

        for e in self.COMPUTE:
            getattr(block, e)(body(e))



BPC = 2
E = 8
KC = 8
D = 1024
CAP = 384
CC = CAP // 128
EPS = 1e-5


def build_nc(T=2048, QT=1024):
    from contextlib import ExitStack

    NT = QT // 128
    NQ = (BPC * T) // QT
    QPB = T // QT
    NL = E * CC
    W5 = 5
    NB = E * CC

    nc = bass.Bass("TRN2", target_bir_lowering=False)

    xh = nc.dram_tensor("x", [BPC, T, D], F32, kind="ExternalInput").ap()
    chh = nc.dram_tensor("c", [BPC, D], F32, kind="ExternalInput").ap()
    gfh = nc.dram_tensor("gfmh", [128, KC * E], F32, kind="ExternalInput").ap()
    wth = nc.dram_tensor("WT", [E, D, D], F32, kind="ExternalInput").ap()
    bbeh = nc.dram_tensor("BBEh", [E, 128, D], F32, kind="ExternalInput").ap()
    wmh = nc.dram_tensor("wmod", [D, 6 * D], F32, kind="ExternalInput").ap()
    trh = nc.dram_tensor("TRIUh", [128, 128], F32, kind="ExternalInput").ap()
    rch = nc.dram_tensor("RECTh", [128, T // 128], F32, kind="ExternalInput").ap()
    idh = nc.dram_tensor("IDNh", [128, 128], F32, kind="ExternalInput").ap()
    onh = nc.dram_tensor("ONESh", [128, 128], F32, kind="ExternalInput").ap()
    suh = nc.dram_tensor("SUTh", [128, 128], F32, kind="ExternalInput").ap()
    ich = nc.dram_tensor("IOTACh", [128, CAP], F32, kind="ExternalInput").ap()
    ibh = nc.dram_tensor("IOTBh", [128, 1], F32, kind="ExternalInput").ap()
    eofh = nc.dram_tensor("EOFFh", [128, E], F32, kind="ExternalInput").ap()
    out = nc.dram_tensor("out", [BPC, T, D], F32, kind="ExternalOutput").ap()
    h1sv = [nc.dram_tensor(f"h1s{j}", [QT, D], F32, kind="Internal").ap()
            for j in range(2)]
    x1sv = [nc.dram_tensor(f"x1s{j}", [QT, D], F32, kind="Internal").ap()
            for j in range(3)]
    yslabv = [nc.dram_tensor(f"ysl{j}", [E * CAP, D], F32, kind="Internal").ap()
              for j in range(2)]

    es = ExitStack()
    sb = lambda n, s, dt=F32: es.enter_context(nc.sbuf_tensor(n, s, dt))
    ps = lambda n, s: es.enter_context(nc.psum_tensor(n, s, F32))

    triu = sb("triu", [128, 128])
    rect = sb("rect", [128, T // 128])
    idn = sb("idn", [128, 128])
    ones = sb("ones", [128, 128])
    sut = sb("sut", [128, 128])
    iotac = sb("iotac", [128, CAP])
    iotab = sb("iotab", [128, 1])
    eoffc = sb("eoffc", [128, E])
    gfm = sb("gfm", [128, KC * E])
    g2fm = sb("g2fm", [128, KC * E])
    avst = sb("avst", [33, 3 * D])
    bvec = avst[0:2, 2 * D:3 * D]
    gbf = sb("gbf", [128, BPC * KC])
    cb1f = sb("cb1f", [128, BPC * KC])
    sbf = sb("sbf", [128, BPC * KC])
    scfm = sb("scfm", [128, KC * BPC])
    abc = sb("abc", [128, D])
    bbc = sb("bbc", [128, D])
    cb1bc = sb("cb1bc", [128, D])
    sbbc = sb("sbbc", [128, D])
    gbbc = sb("gbbc", [128, 2 * D])
    vvec = avst[0:1, 0:D]
    berow = sb("berow", [1, E])
    cmv = avst[0:1, D:2 * D]
    rcmv = sb("rcmv", [1, E])
    dums = sb("dums", [1, 1])
    xt = sb("xt", [128, 2 * D])
    x1t = sb("x1t", [128, 2 * D])
    lnsh = sb("lnsh", [128, D])
    lnab = sb("lnab", [128, D])
    h1m = sb("h1m", [128, D])
    fmt = sb("fmt", [128, D])
    tsum = sb("tsum", [128, D])
    srd = sb("srd", [128, 2 * D])
    x1rb = sb("x1rb", [128, 2 * D])
    wbuf = sb("wbuf", [128, 2 * KC * D])
    gbuf = sb("gbuf", [128, 2 * D])
    gfmb = sb("gfmb", [128, 2 * D])
    ybuf = sb("ybuf", [128, 4 * D])
    bbx = sb("bbx", [128, 2 * D])
    Sb = sb("Sbuf", [128, 2 * CAP])
    rhsE = sb("rhsE", [128, E * 40])
    pgs = sb("pgs", [40, CAP])
    Tc = sb("Tc", [128, 40 * CC])
    zeros8 = sb("zeros8", [128, E])
    gcombs = sb("gcombs", [128, NL])
    invG = sb("invG", [128, NL], I32)
    gidxI = sb("gidxI", [128, 3 * NT * 2], I32)
    gix0 = sb("gix0", [128, 1])
    gix1 = sb("gix1", [128, 1])
    st6 = sb("st6", [128, 12])
    mv1 = sb("mv1", [128, 2])
    vep = sb("vep", [128, 1])
    sq = sb("sq", [128, 1])
    rstd = sb("rstd", [128, 1])
    negm = sb("negm", [128, 1])
    lgt = sb("lgt", [128, E])
    exb = sb("exb", [128, E])
    nmx = sb("nmx", [128, 1])
    ssum = sb("ssum", [128, 1])
    rsum = sb("rsum", [128, 1])
    probs = sb("probs", [128, E])
    pm1 = sb("pm1", [128, 1])
    pm2 = sb("pm2", [128, 1])
    ge1 = sb("ge1", [128, E])
    msk = sb("msk", [128, E])
    combtm = sb("combtm", [128, E])
    selb = sb("selb", [128, E])
    rkb = sb("rkb", [128, E])
    t1b = sb("t1b", [128, E])
    t2b = sb("t2b", [128, E])
    tmp8 = sb("tmp8s", [128, E])
    iotc = sb("iotc", [128, 1])

    RA, RB_ = 3072, 11264

    def modsl(c0, c1):
        if c1 <= 5120:
            return wbuf[0:2, RA + c0: RA + c1]
        return wbuf[0:2, RB_ + (c0 - 5120): RB_ + (c1 - 5120)]

    sa_s, ca_s, ga_s = modsl(0, D), modsl(D, 2 * D), modsl(2 * D, 3 * D)
    sb_s, cb_s, gb_s = modsl(3 * D, 4 * D), modsl(4 * D, 5 * D), modsl(5 * D, 6 * D)
    cs_sl = wbuf[0:2, RB_ + 1024: RB_ + 2048]
    sc_sl = wbuf[0:2, RB_ + 2048: RB_ + 3072]
    gap = cs_sl
    absc = sc_sl
    gbp = sa_s
    avec = ca_s
    cb1v = ga_s
    av0 = avst[32:33, 0:D]
    av1 = avst[32:33, D:2 * D]

    pA = ps("pA", [128, 512])
    pB = ps("pB", [128, 512])
    pC = ps("pC", [128, 512])
    pD = ps("pD", [128, 512])
    pE = ps("pE", [128, 512])
    pF = ps("pF", [128, 512])
    pG = ps("pG", [128, 512])
    pH = ps("pH", [128, 512])

    p = Prog()
    for cn, eng in (("cc", "sync"), ("cx0", "sync"), ("cx1", "sync"),
                    ("cst", "sync"), ("cav", "sync"), ("cbv", "sync"),
                    ("crx0", "sync"), ("crx1", "sync"),
                    ("w0", "scalar"), ("w1", "scalar"), ("w2", "sync"),
                    ("c1w0", "scalar"), ("c1w1", "scalar"),
                    ("cxw0", "scalar"), ("cxw1", "scalar"),
                    ("bb0", "scalar"), ("bb1", "scalar"),
                    ("g0", "gpsimd"), ("g1", "gpsimd"),
                    ("gr0", "gpsimd"), ("gr1", "gpsimd"),
                    ("ys0", "sync"), ("ys1", "sync"),
                    ("ys2", "sync"), ("ys3", "sync")):
        p.channel(cn, eng)

    for dst, src in ((triu, trh), (rect, rch), (idn, idh), (ones, onh),
                     (sut, suh), (iotac, ich), (iotab, ibh), (eoffc, eofh),
                     (gfm, gfh)):
        p.dma("cc", lambda e, d=dst, s=src: e.dma_start(out=d[:], in_=s[:]))
    p.dma("cc", lambda e: e.dma_start(out=cs_sl, in_=chh[:]))
    CC_ = p.tok("cc")

    t_silu = p.step("scalar", lambda e: e.activation(sc_sl, cs_sl, AF.Silu), [CC_])
    prev_ev = None
    for k in range(KC):
        tm = p.step("tensor", lambda e, k=k: e.transpose(
            pH[:, 0:2], sc_sl[:, k * 128:(k + 1) * 128], idn[0:2, 0:2]),
            [t_silu, CC_, prev_ev])
        prev_ev = p.step("vector", lambda e, k=k: e.tensor_copy(
            scfm[:, k * BPC:(k + 1) * BPC], pH[:, 0:2]), [tm])
    SCFM = prev_ev

    banks6 = (pA, pB, pC, pD, pE, pF)
    mod_ev = []
    wmreg = (wbuf[:, 0:3072], wbuf[:, 8192:8192 + 3072], ybuf[:, 0:3072])
    wmch = ("w0", "w1", "w2")
    wm_gate = {0: None, 1: None, 2: None}
    for jh in range(2):
        mm_last = None
        for k in range(KC):
            r = (jh * KC + k) % 3
            ld = p.dma(wmch[r], lambda e, jh=jh, k=k, r=r: e.dma_start(
                out=wmreg[r],
                in_=wmh[k * 128:(k + 1) * 128, jh * 3072:(jh + 1) * 3072]),
                [wm_gate[r]])
            for j6 in range(6):
                mm_last = p.step("tensor", lambda e, k=k, r=r, j6=j6: e.matmul(
                    banks6[j6][0:2, :], scfm[:, k * BPC:(k + 1) * BPC],
                    wmreg[r][:, j6 * 512:(j6 + 1) * 512],
                    start=(k == 0), stop=(k == KC - 1)),
                    [ld, SCFM] + ([mod_ev[-1]] if (jh == 1 and k == 0 and j6 == 0) else []),
                    noinc=(j6 != 5))
            wm_gate[r] = p.tok("tensor")
        for j6 in range(6):
            c0_ = jh * 3072 + j6 * 512
            mod_ev.append(p.step("vector", lambda e, c0_=c0_, j6=j6: e.tensor_copy(
                modsl(c0_, c0_ + 512), banks6[j6][0:2, :]), [mm_last]))
    MODDONE = mod_ev[-1]
    wslot_gate = [p.tok("tensor"), p.tok("tensor")]

    t1 = p.step("vector", lambda e: e.tensor_scalar(gap, ga_s, 0.0, None, ALU.is_ge), [MODDONE])
    t2 = p.step("vector", lambda e: e.tensor_scalar(gap, gap, 2.0, -1.0, ALU.mult, ALU.add), [t1])
    t3a = p.step("scalar", lambda e: e.activation(absc, ga_s, AF.Abs), [MODDONE])
    t3 = p.step("vector", lambda e: e.tensor_scalar(absc, absc, 0.25, None, ALU.max), [t3a])
    t4 = p.step("vector", lambda e: e.tensor_tensor(gap, gap, absc, ALU.mult), [t2, t3])
    tB = p.step("vector", lambda e: e.tensor_tensor(bvec[:], sa_s, gap, ALU.mult), [t4, MODDONE])
    t5 = p.step("vector", lambda e: e.tensor_scalar(gbp, gb_s, 0.0, None, ALU.is_ge), [MODDONE, tB])
    t6 = p.step("vector", lambda e: e.tensor_scalar(gbp, gbp, 2.0, -1.0, ALU.mult, ALU.add), [t5])
    t7a = p.step("scalar", lambda e: e.activation(absc, gb_s, AF.Abs), [MODDONE, t4])
    t7 = p.step("vector", lambda e: e.tensor_scalar(absc, absc, 0.25, None, ALU.max), [t7a])
    t8 = p.step("vector", lambda e: e.tensor_tensor(gbp, gbp, absc, ALU.mult), [t6, t7])
    t9 = p.step("vector", lambda e: e.tensor_scalar(avec, ca_s, 1.0, None, ALU.add), [MODDONE])
    tA = p.step("vector", lambda e: e.tensor_tensor(avec, avec, gap, ALU.mult), [t9, t4])
    tC = p.step("vector", lambda e: e.tensor_scalar(cb1v, cb_s, 1.0, None, ALU.add), [MODDONE, t1, t3a])
    tv0 = p.dma("cav", lambda e: e.dma_start(out=av0, in_=avec[0:1, :]), [tA])
    tv1 = p.dma("cav", lambda e: e.dma_start(out=av1, in_=avec[1:2, :]), [tA, tv0])
    tz8 = p.step("vector", lambda e: e.memset(zeros8[:], 0.0), [])
    trE = p.step("vector", lambda e: e.memset(rhsE[:], 0.0), [])
    tr5 = p.step("vector", lambda e: e.memset(rhsE[:, 4:E * 40:45], 1.0), [trE])

    prev_ev = SCFM
    for (srcv, dstf, dep0) in ((cb1v, cb1f, tC), (sb_s, sbf, MODDONE),
                               (gbp, gbf, t8)):
        for k in range(KC):
            tm = p.step("tensor", lambda e, srcv=srcv, k=k: e.transpose(
                pH[:, 4:6], srcv[:, k * 128:(k + 1) * 128], idn[0:2, 0:2]),
                [dep0, prev_ev])
            prev_ev = p.step("vector", lambda e, dstf=dstf, k=k: e.tensor_copy(
                dstf[:, k * BPC:(k + 1) * BPC], pH[:, 4:6]), [tm])
    FMDONE = prev_ev
    ph_a_t = p.tok("tensor")
    ph_a_v = p.tok("vector")

    xt_free = [None, None]
    x1t_free = [None, None]
    misc = {}
    state = {}

    def emit_boundary(b):
        cm0 = p.step("vector", lambda e: e.memset(cmv[0:1, :], 0.0),
                     [misc.get("cmv_r"), misc.get("cmv_w")])
        misc["cmv_w"] = cm0
        prev = None
        for h in range(2):
            bank = (pE, pF)[h]
            tmm = p.step("tensor", lambda e, b=b, h=h, bank=bank: e.matmul(
                bank[:, 0:512], ones[32:33, :],
                (av0 if b == 0 else av1)[:, h * 512:(h + 1) * 512],
                start=True, stop=True),
                [tv1, CC_, FMDONE, misc.get("carry_upd"), misc.get("rk_r"),
                 prev])
            prev = p.step("vector", lambda e, h=h, bank=bank: e.tensor_copy(
                abc[:, h * 512:(h + 1) * 512], bank[:, 0:512]), [tmm])
        tvb = p.dma("cbv", lambda e, b=b: e.dma_start(
            out=vvec[0:1, :], in_=bvec[b:b + 1, :]),
            [tB, misc.get("vvec_r"), prev])
        for h in range(2):
            bank = (pE, pF)[h]
            tmm = p.step("tensor", lambda e, h=h, bank=bank: e.matmul(
                bank[:, 0:512], ones[0:1, :], vvec[0:1, h * 512:(h + 1) * 512],
                start=True, stop=True), [tvb, prev])
            prev = p.step("vector", lambda e, h=h, bank=bank: e.tensor_copy(
                bbc[:, h * 512:(h + 1) * 512], bank[:, 0:512]), [tmm])
        for (vecf, base) in ((cb1f, cb1bc), (sbf, sbbc), (gbf, None)):
            vts = []
            for k in range(KC):
                bank = (pE, pF)[k // 4]
                off = (k % 4) * 128
                vts.append(p.step("tensor", lambda e, vecf=vecf, k=k, b=b, bank=bank, off=off: e.transpose(
                    bank[0:1, off:off + 128],
                    vecf[:, k * BPC + b: k * BPC + b + 1], idn[:]),
                    [FMDONE, prev]))
            e0 = p.step("vector", lambda e: e.tensor_copy(
                vvec[0:1, 0:512], pE[0:1, 0:512]), [vts[3], prev])
            e1 = p.step("vector", lambda e: e.tensor_copy(
                vvec[0:1, 512:1024], pF[0:1, 0:512]), [vts[7], e0])
            for h in range(2):
                bank = (pE, pF)[h]
                if base is None:
                    dst = gbbc[:, b * D + h * 512: b * D + (h + 1) * 512]
                else:
                    dst = base[:, h * 512:(h + 1) * 512]
                tmm = p.step("tensor", lambda e, h=h, bank=bank: e.matmul(
                    bank[:, 0:512], ones[0:1, :],
                    vvec[0:1, h * 512:(h + 1) * 512],
                    start=True, stop=True), [e0, e1])
                prev = p.step("vector", lambda e, bank=bank, dst=dst: e.tensor_copy(
                    dst, bank[:, 0:512]), [tmm])
        misc["vvec_r"] = prev
        g2t = None
        for k in range(KC):
            g2t = p.step("vector", lambda e, k=k, b=b: e.tensor_scalar(
                g2fm[:, k * E:(k + 1) * E], gfm[:, k * E:(k + 1) * E],
                cb1f[:, k * BPC + b: k * BPC + b + 1], None, ALU.mult),
                [CC_, FMDONE, misc.get("g2_r"), g2t])
        misc["G2"] = g2t
        bem = None
        for k in range(KC):
            bem = p.step("tensor", lambda e, k=k, b=b: e.matmul(
                pF[0:1, 0:E], sbf[:, k * BPC + b: k * BPC + b + 1],
                gfm[:, k * E:(k + 1) * E],
                start=(k == 0), stop=(k == KC - 1)),
                [FMDONE, CC_, prev] if k == 0 else ([FMDONE] if k == KC - 1 else []),
                noinc=(k != KC - 1))
        misc["BER"] = p.step("vector", lambda e: e.tensor_copy(
            berow[0:1, :], pF[0:1, 0:E]), [bem])
        misc["ABC"] = misc["BER"]

    def ln_chain(src_ap, deps_in):
        tst0 = p.step("vector", lambda e: e.bn_stats(
            st6[:, 0:6], src_ap[:, 0:512]), deps_in + [misc.get("aggr")])
        tst = p.step("vector", lambda e: e.bn_stats(
            st6[:, 6:12], src_ap[:, 512:1024]), deps_in + [tst0])
        tag = p.step("vector", lambda e: e.bn_aggr(mv1[:], st6[:]),
                     [tst0, tst, misc.get("vep"), misc.get("ln_s")])
        misc["aggr"] = tag
        tve = p.step("vector", lambda e: e.tensor_scalar(
            vep[:], mv1[:, 1:2], EPS, None, ALU.add), [tag, misc.get("sq_s")])
        misc["vep"] = tve
        tsq = p.step("scalar", lambda e: e.activation(sq[:], vep[:], AF.Sqrt),
                     [tve, misc.get("rstd_w")])
        misc["sq_s"] = tsq
        trs = p.step("vector", lambda e: e.reciprocal(rstd[:], sq[:]),
                     [tsq, misc.get("ln_s")])
        misc["rstd_w"] = trs
        app = p.step("vector", lambda e: e.tensor_scalar(
            lnsh[:], src_ap, mv1[:, 0:1], rstd[:], ALU.subtract, ALU.mult),
            deps_in + [trs, tag, misc.get("lnsh_r"), misc.get("lnsh_r2")])
        misc["ln_s"] = app
        return app

    def emit_xld(q, i):
        s = i % 2
        state[("xld", q, i)] = p.dma(("cx0", "cx1")[s], lambda e, q=q, i=i, s=s: e.dma_start(
            out=xt[:, s * D:(s + 1) * D],
            in_=xh[q // QPB, ((q % QPB) * NT + i) * 128:((q % QPB) * NT + i + 1) * 128, :]),
            [xt_free[s]])

    def tile_gen(q, i):
        b = q // QPB
        par = q % 2
        gq = (q % QPB) * NT + i
        s = i % 2
        xp = i % 2
        if i == 0 and q % QPB == 0:
            emit_boundary(b)
        if i == 0:
            rc0 = p.step("vector", lambda e: e.memset(rcmv[0:1, :], 0.0),
                         [misc.get("rk_r"), misc.get("rk_w")])
            misc["rk_w"] = rc0
        ld = state[("xld", q, i)]
        tln = ln_chain(xt[:, s * D:(s + 1) * D], [ld])
        tlnA = p.step("vector", lambda e: e.tensor_tensor(
            lnab[:], lnsh[:], abc[:], ALU.mult),
            [tln, misc.get("ABC"), misc.get("cum_mm")])
        misc["lnsh_r"] = tlnA
        yield
        cms = []
        for h in range(2):
            bank = (pE, pH)[h]
            p.step("tensor", lambda e, h=h, bank=bank: e.matmul(
                bank[:, 0:512], triu[:],
                lnab[:, h * 512:(h + 1) * 512], start=True, stop=False),
                [tlnA, CC_, misc.get("x1_add"),
                 (misc.get("lnsh_r") if h == 1 else None)], noinc=True)
            cm = p.step("tensor", lambda e, h=h, bank=bank: e.matmul(
                bank[:, 0:512], ones[0:1, :],
                cmv[0:1, h * 512:(h + 1) * 512], start=False, stop=True),
                [misc.get("cmv_w")])
            cms.append(cm)
            misc["cum_mm"] = cm
        tot0 = p.step("tensor", lambda e: e.matmul(
            pF[0:1, 0:512], ones[:, 0:1], lnab[:, 0:512],
            start=True, stop=True),
            [tlnA, CC_, misc.get("carry_upd"), misc.get("rk_r")])
        misc["cmv_r"] = cms[1]
        yield
        xf = x1t_free[xp]
        adds = []
        for h in range(2):
            bank = (pE, pH)[h]
            a1 = p.step("vector", lambda e, gq=gq, h=h, s=s, xp=xp, bank=bank: e.scalar_tensor_tensor(
                x1t[:, xp * D + h * 512: xp * D + (h + 1) * 512],
                bank[:, 0:512], rect[:, gq:gq + 1],
                xt[:, s * D + h * 512: s * D + (h + 1) * 512], ALU.mult, ALU.add),
                [cms[h], ld, (xf[0] if xf else None), CC_])
            if h == 0 and xf:
                state[("x1cov",) + xf[1]] = a1
            a2 = p.step("vector", lambda e, h=h, xp=xp: e.tensor_tensor(
                x1t[:, xp * D + h * 512: xp * D + (h + 1) * 512],
                x1t[:, xp * D + h * 512: xp * D + (h + 1) * 512],
                bbc[:, h * 512:(h + 1) * 512], ALU.add),
                [a1, misc.get("ABC")])
            adds.append(a2)
            misc["x1_add"] = a2
            if h == 0:
                cu0 = p.step("vector", lambda e: e.tensor_tensor(
                    cmv[0:1, 0:512], cmv[0:1, 0:512],
                    pF[0:1, 0:512], ALU.add), [tot0, cms[0], misc.get("cmv_w")])
                misc["cmv_w"] = cu0
        cxw = ("cxw0", "cxw1")[xp]
        x1w = p.dma(cxw, lambda e, i=i, q=q, xp=xp: e.dma_start(
            out=x1sv[q % 3][i * 128:(i + 1) * 128, :], in_=x1t[:, xp * D:(xp + 1) * D]),
            [adds[1], state.get(("rbx", q % 3))])
        x1t_free[xp] = (x1w, (q, i))
        xt_free[s] = adds[1]
        tot1 = p.step("tensor", lambda e: e.matmul(
            pF[0:1, 0:512], ones[:, 0:1], lnab[:, 512:1024],
            start=True, stop=True),
            [tlnA, CC_, cu0, misc.get("rk_r")])
        if i + 1 < NT:
            emit_xld(q, i + 1)
        yield
        cu1 = p.step("vector", lambda e: e.tensor_tensor(
            cmv[0:1, 512:1024], cmv[0:1, 512:1024],
            pF[0:1, 0:512], ALU.add), [tot1, cms[1], misc.get("cmv_w")])
        misc["carry_upd"] = cu1
        misc["cmv_w"] = cu1
        th1 = ln_chain(x1t[:, xp * D:(xp + 1) * D], [adds[0], adds[1]])
        m1t = p.step("vector", lambda e: e.tensor_tensor(
            h1m[:], lnsh[:], cb1bc[:], ALU.mult),
            [th1, misc.get("vvec_r"), misc.get("h1m_w")])
        m2t = p.step("vector", lambda e: e.tensor_tensor(
            h1m[:], h1m[:], sbbc[:], ALU.add), [m1t])
        misc["lnsh_r2"] = m2t
        misc["h1m_w"] = p.dma(("c1w0", "c1w1")[par], lambda e, i=i, par=par: e.dma_start(
            out=h1sv[par][i * 128:(i + 1) * 128, :], in_=h1m[:]),
            [m2t] + state.get(("gath", par), []))
        fme = []
        for k in range(KC):
            bank = (pH, pE)[k % 2]
            off = (k // 2) * 128
            tmk = p.step("tensor", lambda e, k=k, bank=bank, off=off: e.transpose(
                bank[:, off:off + 128], lnsh[:, k * 128:(k + 1) * 128], idn[:]),
                [th1, misc.get(("fme", k % 2)),
                 (misc.get("x1_add") if k < 2 else None)])
            fme.append(p.step("vector", lambda e, k=k, bank=bank, off=off: e.tensor_copy(
                fmt[:, k * 128:(k + 1) * 128], bank[:, off:off + 128]),
                [tmk, misc.get("lg_mm")]))
            misc[("fme", k % 2)] = fme[-1]
        misc["lnsh_r"] = p.tok("tensor")
        for k in range(KC):
            p.step("tensor", lambda e, k=k: e.matmul(
                pF[:, 0:E], fmt[:, k * 128:(k + 1) * 128],
                g2fm[:, k * E:(k + 1) * E], start=(k == 0), stop=False),
                [fme[k], misc.get("G2")] + ([misc.get("lg_ev"), misc.get("carry_upd"),
                                             misc.get("rk_r")] if k == 0 else []),
                noinc=True)
        lgm = p.step("tensor", lambda e: e.matmul(
            pF[:, 0:E], ones[0:1, :], berow[0:1, :], start=False, stop=True),
            [misc.get("BER"), fme[-1]])
        misc["lg_mm"] = lgm
        lge = p.step("vector", lambda e: e.tensor_copy(lgt[:], pF[:, 0:E]),
                     [lgm, misc.get("smx")])
        misc["lg_ev"] = lge
        trm = p.step("vector", lambda e: e.reduce_max(pm1[:], lgt[:], axis=AX.X),
                     [lge, misc.get("t18")])
        tnr = p.step("vector", lambda e: e.tensor_scalar(
            nmx[:], pm1[:], -1.0, None, ALU.mult), [trm, misc.get("exp_s")])
        tex = p.step("scalar", lambda e: e.activation(
            exb[:], lgt[:], AF.Exp, bias=nmx[:], scale=f32(1.0)),
            [lge, tnr, misc.get("probs_r")])
        misc["exp_s"] = tex
        tsu = p.step("vector", lambda e: e.reduce_sum(ssum[:], exb[:], axis=AX.X),
                     [tex])
        trc = p.step("vector", lambda e: e.reciprocal(rsum[:], ssum[:]), [tsu])
        tpr = p.step("vector", lambda e: e.tensor_scalar(
            probs[:], exb[:], rsum[:], None, ALU.mult),
            [trc, tex, misc.get("probs_r")])
        tm1 = p.step("vector", lambda e: e.reduce_max(pm1[:], probs[:], axis=AX.X),
                     [tpr, trm, tnr])
        tg1 = p.step("vector", lambda e: e.tensor_scalar(
            ge1[:], probs[:], pm1[:], None, ALU.is_ge), [tm1, tpr])
        tms = p.step("vector", lambda e: e.scalar_tensor_tensor(
            msk[:], ge1[:], -1e30, probs[:], ALU.mult, ALU.add), [tg1, tpr])
        tm2 = p.step("vector", lambda e: e.reduce_max(pm2[:], msk[:], axis=AX.X),
                     [tms])
        tg2 = p.step("vector", lambda e: e.tensor_scalar(
            ge1[:], probs[:], pm2[:], None, ALU.is_ge), [tm2, tpr, tms, tg1])
        tcb = p.step("vector", lambda e: e.tensor_tensor(
            combtm[:], ge1[:], probs[:], ALU.mult), [tg2, tpr])
        misc["smx"] = tcb
        tsl = p.step("vector", lambda e: e.tensor_scalar(
            selb[:], combtm[:], 0.0, None, ALU.is_gt), [tcb, misc.get("rk_mm")])
        t18 = p.step("vector", lambda e: e.tensor_scalar(
            t1b[:], probs[:], pm1[:], None, ALU.is_ge), [tpr, tm1])
        misc["t18"] = t18
        misc["probs_r"] = t18
        yield
        rk1 = p.step("tensor", lambda e: e.matmul(
            pF[:, 32:40], sut[:], selb[:], start=True, stop=False),
            [tsl, CC_, misc.get("rkb_w"), lgm], noinc=True)
        rk2 = p.step("tensor", lambda e: e.matmul(
            pF[:, 32:40], ones[0:1, :], rcmv[0:1, :], start=False, stop=True),
            [misc.get("rk_w"), tsl])
        rk3 = p.step("tensor", lambda e: e.matmul(
            pF[0:1, 48:56], ones[:, 0:1], selb[:], start=True, stop=True),
            [tsl, misc.get("rk_r")])
        misc["rk_mm"] = rk3
        trk = p.step("vector", lambda e: e.tensor_scalar(
            rkb[:], pF[:, 32:40], 1.0, None, ALU.add), [rk2, rk3, misc.get("S_w")])
        trk2 = p.step("vector", lambda e: e.tensor_tensor(
            rkb[:], rkb[:], selb[:], ALU.mult), [trk, tsl])
        trk3 = p.step("vector", lambda e: e.tensor_scalar(
            rkb[:], rkb[:], -1.0, None, ALU.add), [trk2])
        misc["rkb_w"] = trk3
        rcu = p.step("vector", lambda e: e.tensor_tensor(
            rcmv[0:1, :], rcmv[0:1, :], pF[0:1, 48:56], ALU.add),
            [rk3, rk2, misc.get("rk_w")])
        misc["rk_w"] = rcu
        misc["rk_r"] = rcu
        tic = p.step("vector", lambda e, i=i: e.tensor_scalar(
            iotc[:], iotab[:], float(i * 128), None, ALU.add), [CC_])
        t28 = p.step("vector", lambda e: e.tensor_tensor(
            t2b[:], selb[:], t1b[:], ALU.subtract), [t18, tsl])
        tt8 = p.step("vector", lambda e: e.tensor_tensor(
            tmp8[:], eoffc[:], rkb[:], ALU.add), [trk3, CC_])
        qs3 = (q % 3) * NT * 2
        for (selm, gx, chv) in ((t1b, gix0, 0), (t2b, gix1, 1)):
            gm = p.step("vector", lambda e, selm=selm: e.tensor_tensor(
                msk[:], selm[:], tmp8[:], ALU.mult), [tt8, t28])
            gr = p.step("vector", lambda e, gx=gx: e.reduce_sum(
                gx[:], msk[:], axis=AX.X), [gm])
            gc = p.step("vector", lambda e, gx=gx: e.tensor_scalar(
                gx[:], gx[:], float(E * CAP - 1), None, ALU.min), [gr])
            p.step("vector", lambda e, gx=gx, col=qs3 + i * 2 + chv: e.tensor_copy(
                gidxI[:, col:col + 1], gx[:]), [gc])
        ra0 = p.step("vector", lambda e: e.tensor_scalar(
            rhsE[:, 0:E * 40:45], zeros8[:], iotc[:], None, ALU.add),
            [tic, tz8, trE, misc.get("rhs_mm")])
        ra1 = p.step("vector", lambda e: e.tensor_copy(
            rhsE[:, 1:E * 40:45], combtm[:]), [tcb, ra0])
        misc["RHS"] = ra1
        yield
        RHS = misc["RHS"]
        for ei in range(E):
            ssl = (ei % 2) * CAP
            tS = p.step("vector", lambda e, ei=ei, ssl=ssl: e.tensor_scalar(
                Sb[:, ssl:ssl + CAP], iotac[:], rkb[:, ei:ei + 1], None,
                ALU.is_equal),
                [trk3, CC_, misc.get(("S_mm", ei % 2))])
            misc["S_w"] = tS
            first = (i == 0 and ei == 0)
            last = (i == NT - 1 and ei == E - 1)
            p.step("tensor", lambda e, ei=ei, ssl=ssl, first=first, last=last: e.matmul(
                pG[0:40, 0:CAP],
                rhsE[:, ei * 40:(ei + 1) * 40],
                Sb[:, ssl:ssl + CAP],
                start=first, stop=last),
                [tS, RHS, tr5] + ([misc.get("pG_free")] if first else []))
            misc[("S_mm", ei % 2)] = p.tok("tensor")
        misc["rhs_mm"] = p.tok("tensor")

    def emit_idx(q):
        SLOTS_MM = p.tok("tensor")
        SCAL = p.tok("scalar")
        gpg = [Tok("g0", p.count["g0"]), Tok("g1", p.count["g1"])]
        evac = p.step("vector", lambda e: e.tensor_copy(
            pgs[:], pG[0:40, 0:CAP]), [SLOTS_MM, misc.get("pgs_r")])
        tcs = []
        for cc in range(CC):
            bank = (pG, pH)[cc % 2]
            tt = p.step("tensor", lambda e, cc=cc, bank=bank: e.transpose(
                bank[:, 0:40],
                pgs[0:40, cc * 128:(cc + 1) * 128], idn[0:40, 0:40]),
                [evac, (tcs[cc - 2] if cc >= 2 else None)])
            tcs.append(p.step("vector", lambda e, cc=cc, bank=bank: e.tensor_copy(
                Tc[:, cc * 40:(cc + 1) * 40], bank[:, 0:40]),
                [tt]))
        misc["pgs_r"] = p.tok("tensor")
        ev1 = ev2 = None
        for cc in range(CC):
            ev1 = p.step("vector", lambda e, cc=cc: e.tensor_copy(
                gcombs[:, cc:NL:CC], Tc[:, cc * 40 + 1: (cc + 1) * 40: 5]),
                [tcs[cc], SCAL, ev1])
            ev2 = p.step("vector", lambda e, cc=cc: e.tensor_copy(
                invG[:, cc:NL:CC], Tc[:, cc * 40: (cc + 1) * 40: 5]),
                [tcs[cc]] + (gpg if cc == 0 else [ev2]))
        state[("IDX", q)] = p.tok("vector")
        state[("ev1", q)] = ev1
        misc["pG_free"] = state[("IDX", q)]

    def emit_gather(q, j, blocks):
        par = q % 2
        ei, cc = blocks[j]
        gp_ = j % 2
        gsl = gp_ * D
        state[("gath_t", q, j)] = (p.dma(("g0", "g1")[gp_], lambda e, ei=ei, cc=cc, gsl=gsl, par=par: e.indirect_dma_start(
            out=gbuf[:, gsl:gsl + D], out_offset=None,
            in_=h1sv[par][:, :],
            in_offset=IndirectOffsetOnAxis(invG[:, ei * CC + cc: ei * CC + cc + 1], axis=0)),
            [state[("H1W", q)], state[("IDX", q)], misc.get(("gb_free", gp_))]), gsl)

    def emit_tsc(q, j, blocks):
        ei, cc = blocks[j]
        gt, gsl = state[("gath_t", q, j)]
        state[("tsc", q, j)] = p.step("scalar", lambda e, ei=ei, cc=cc, gsl=gsl: e.activation(
            gbuf[:, gsl:gsl + D], gbuf[:, gsl:gsl + D], AF.Identity,
            bias=f32(0.0), scale=gcombs[:, ei * CC + cc: ei * CC + cc + 1]),
            [gt, state[("ev1", q)]])

    def emit_wload(ei):
        wsl = ei % 2
        wl = None
        extra = misc.pop("wsg0", [])
        for lh in range(2):
            wl = p.dma(("w0", "w1")[wsl], lambda e, ei=ei, wsl=wsl, lh=lh: e.dma_start(
                out=wbuf[:, wsl * 8192 + lh * 4096: wsl * 8192 + (lh + 1) * 4096]
                    .rearrange("p (k o) -> p k o", o=D),
                in_=wth[ei, lh * 512:(lh + 1) * 512, :]
                    .rearrange("(k p) o -> p k o", p=128)),
                ([wslot_gate[wsl]] + extra) if lh == 0 else [])
        misc[("wl", wsl)] = wl
        misc[("bbl_t", wsl)] = p.dma(("bb0", "bb1")[wsl], lambda e, ei=ei, wsl=wsl: e.dma_start(
            out=bbx[:, wsl * D:(wsl + 1) * D], in_=bbeh[ei]),
            [misc.get(("bbx_free", wsl))])

    def emit_moe_block(q, j, blocks):
        ei, cc = blocks[j]
        wl = misc[("wl", ei % 2)]
        gt, gsl = state[("gath_t", q, j)]
        gp_ = j % 2
        tsc = state.pop(("tsc", q, j))
        tps = []
        for k in range(KC):
            bank = (pC, pD)[k % 2]
            off = (k // 2) * 128
            tmk = p.step("tensor", lambda e, k=k, gsl=gsl, bank=bank, off=off: e.transpose(
                bank[:, off:off + 128], gbuf[:, gsl + k * 128: gsl + (k + 1) * 128],
                idn[:]),
                [tsc, misc.get(("gf_ev", k % 2))])
            tps.append(p.step("vector", lambda e, k=k, gsl=gsl, bank=bank, off=off: e.tensor_copy(
                gfmb[:, gsl + k * 128: gsl + (k + 1) * 128], bank[:, off:off + 128]),
                [tmk, misc.get("moe_mm")]))
            misc[("gf_ev", k % 2)] = tps[-1]
        misc[("gb_free", gp_)] = p.tok("tensor")
        if j + 2 < len(blocks):
            emit_gather(q, j + 2, blocks)
        if j + 1 < len(blocks):
            emit_tsc(q, j + 1, blocks)
        for og in range(2):
            bank = (pA, pB)[og]
            mmt = None
            for k in range(KC):
                mmt = p.step("tensor", lambda e, k=k, gsl=gsl, wsl=ei % 2, og=og, bank=bank: e.matmul(
                    bank[:, 0:512],
                    gfmb[:, gsl + k * 128: gsl + (k + 1) * 128],
                    wbuf[:, wsl * 8192 + k * D + og * 512: wsl * 8192 + k * D + og * 512 + 512],
                    start=(k == 0), stop=(k == KC - 1)),
                    [tps[k]] + ([wl, misc.get(("pm_free", og))] if k == 0 else []),
                    noinc=(k != KC - 1))
            yev = p.step("vector", lambda e, j=j, og=og, bank=bank, slE=ei % 2, col=ei * CC + cc: e.scalar_tensor_tensor(
                ybuf[:, (j % 4) * D + og * 512: (j % 4) * D + (og + 1) * 512],
                bbx[:, slE * D + og * 512: slE * D + (og + 1) * 512],
                gcombs[:, col:col + 1],
                bank[:, 0:512], ALU.mult, ALU.add),
                [mmt, misc.get(("bbl_t", ei % 2)),
                 Tok("ys%d" % (j % 4), p.count["ys%d" % (j % 4)]),
                 misc.get("moe_mm")])
            misc["moe_mm"] = yev
            misc[("pm_free", og)] = yev
        par = q % 2
        p.dma("ys%d" % (j % 4), lambda e, ei=ei, cc=cc, j=j, par=par: e.dma_start(
            out=yslabv[par][(ei * CC + cc) * 128:(ei * CC + cc + 1) * 128, :],
            in_=ybuf[:, (j % 4) * D:(j % 4 + 1) * D]),
            [misc["moe_mm"], state.get(("ysl_free", par))])
        if cc == CC - 1:
            wslot_gate[ei % 2] = p.tok("tensor")
            misc[("bbx_free", ei % 2)] = misc["moe_mm"]
            if ei + 2 < E:
                emit_wload(ei + 2)
            elif q + 1 < NQ:
                emit_wload(ei + 2 - E)

    def rb_load01(qr, s2):
        par = qr % 2
        qs3 = (qr % 3) * NT * 2
        state[("rb_r0", s2)] = p.dma("gr0", lambda e, i=s2, par=par, qs3=qs3: e.indirect_dma_start(
            out=srd[:, 0:D], out_offset=None,
            in_=yslabv[par][:, :],
            in_offset=IndirectOffsetOnAxis(gidxI[:, qs3 + i * 2: qs3 + i * 2 + 1], axis=0)),
            [state.get("SFLt"), misc.get("rb_ts1")])
        state[("rb_r1", s2)] = p.dma("gr1", lambda e, i=s2, par=par, qs3=qs3: e.indirect_dma_start(
            out=srd[:, D:2 * D], out_offset=None,
            in_=yslabv[par][:, :],
            in_offset=IndirectOffsetOnAxis(gidxI[:, qs3 + i * 2 + 1: qs3 + i * 2 + 2], axis=0)),
            [state.get("SFLt"), misc.get("rb_ts1")])

    def rb_loadx(qr, s2, tail):
        if tail:
            cov = [Tok("cxw0", p.count["cxw0"]), Tok("cxw1", p.count["cxw1"])]
        else:
            cov = [state.get(("x1cov", qr, s2))]
        state[("rb_rx", s2)] = p.dma(("crx0", "crx1")[s2 % 2],
                                     lambda e, i=s2, qr=qr: e.dma_start(
            out=x1rb[:, (i % 2) * D:(i % 2 + 1) * D],
            in_=x1sv[qr % 3][i * 128:(i + 1) * 128, :]),
            cov + [misc.get("rb_to")])

    def rb_slot(qr, s, tail=False):
        b = qr // QPB
        ts1 = p.step("vector", lambda e: e.tensor_tensor(
            tsum[:], srd[:, 0:D], srd[:, D:2 * D], ALU.add),
            [state[("rb_r0", s)], state[("rb_r1", s)],
             Tok("cst", p.count["cst"])])
        misc["rb_ts1"] = ts1
        if s + 1 < NT:
            rb_load01(qr, s + 1)
        to = None
        for og in range(2):
            tg = p.step("vector", lambda e, og=og, b=b: e.tensor_tensor(
                tsum[:, og * 512:(og + 1) * 512], tsum[:, og * 512:(og + 1) * 512],
                gbbc[:, b * D + og * 512: b * D + (og + 1) * 512], ALU.mult),
                [ts1, misc.get("vvec_r")])
            to = p.step("vector", lambda e, og=og, s=s: e.tensor_tensor(
                tsum[:, og * 512:(og + 1) * 512], tsum[:, og * 512:(og + 1) * 512],
                x1rb[:, (s % 2) * D + og * 512: (s % 2) * D + og * 512 + 512],
                ALU.add), [tg, state[("rb_rx", s)]])
        misc["rb_to"] = to
        gq = (qr % QPB) * NT + s
        p.dma("cst", lambda e, b=b, gq=gq: e.dma_start(
            out=out[b, gq * 128:(gq + 1) * 128, :], in_=tsum[:]), [to])
        if s + 2 < NT:
            rb_loadx(qr, s + 2, tail)
        if s == NT - 1:
            state[("ysl_free", qr % 2)] = ts1
            state[("rbx", qr % 3)] = to

    def rb_prime(qr, tail=False):
        rb_load01(qr, 0)
        rb_loadx(qr, 0, tail)
        if NT > 1:
            rb_loadx(qr, 1, tail)

    blocks = [(ei, cc) for ei in range(E) for cc in range(CC)]
    misc["wsg0"] = [FMDONE, tv1, ph_a_t, ph_a_v]
    emit_xld(0, 0)
    for i in range(NT):
        for _ in tile_gen(0, i):
            pass
    emit_idx(0)
    state[("H1W", 0)] = Tok("c1w0", p.count["c1w0"])
    for q in range(NQ):
        par = q % 2
        if q == 0:
            emit_wload(0)
            emit_wload(1)
        emit_gather(q, 0, blocks)
        emit_gather(q, 1, blocks)
        emit_tsc(q, 0, blocks)
        tgens = {}
        if q + 1 < NQ:
            emit_xld(q + 1, 0)
            tgens[0] = tile_gen(q + 1, 0)
        if q >= 1:
            rb_prime(q - 1)
        if 0 in tgens:
            next(tgens[0])
        for j in range(NB):
            ti, ph = j // 3, j % 3
            g = tgens.get(ti)
            if ph == 0 and g is not None:
                next(g, None)
            emit_moe_block(q, j, blocks)
            if ph == 0 and g is not None:
                next(g, None)
            elif ph == 1:
                if g is not None:
                    next(g, None)
            elif ph == 2:
                if g is not None:
                    next(g, None)
                    next(g, None)
                    del tgens[ti]
                if q + 1 < NQ and ti + 1 < NT:
                    tgens[ti + 1] = tile_gen(q + 1, ti + 1)
                    next(tgens[ti + 1])
                if q >= 1 and ti < NT:
                    rb_slot(q - 1, ti)
        state[("gath", par)] = [Tok("g0", p.count["g0"]), Tok("g1", p.count["g1"])]
        state["SFLt"] = p.step("vector", lambda e: e.memset(dums[:], 0.0),
                               [Tok("ys%d" % k_, p.count["ys%d" % k_])
                                for k_ in range(4)])
        if q + 1 < NQ:
            emit_idx(q + 1)
            cql = ("c1w0", "c1w1")[(q + 1) % 2]
            state[("H1W", q + 1)] = Tok(cql, p.count[cql])
        misc["g2_r"] = p.tok("tensor")
    rb_prime(NQ - 1, tail=True)
    for s in range(NT):
        rb_slot(NQ - 1, s, tail=True)

    p.wait("sync", [p.tok("cst")])
    p.wait("gpsimd", [p.tok("cst")])

    p.alloc_sems(es, nc)
    with es:
        with nc.Block() as block:
            p.replay(block)
    return nc




def make_consts(T, QT):
    TRIU = np.triu(np.ones((128, 128), dtype=np.float32))
    RECT = np.zeros((128, T // 128), dtype=np.float32)
    for i in range(T // 128):
        RECT[:, i] = (1.0 / (i * 128 + np.arange(128) + 1.0)).astype(np.float32)
    IDN = np.eye(128, dtype=np.float32)
    ONES = np.ones((128, 128), dtype=np.float32)
    SUT = np.triu(np.ones((128, 128), dtype=np.float32), 1)
    IOTAC = np.broadcast_to(np.arange(CAP, dtype=np.float32), (128, CAP)).copy()
    IOTB = np.arange(128, dtype=np.float32).reshape(128, 1).copy()
    EOFF = np.broadcast_to(CAP * np.arange(E, dtype=np.float32), (128, E)).copy()
    return TRIU, RECT, IDN, ONES, SUT, IOTAC, IOTB, EOFF


def make_in_maps(x, gate, ew, eb, c, wmod, T, QT, n_cores=8):
    WT = np.ascontiguousarray(ew.transpose(0, 2, 1))
    gfm = np.ascontiguousarray(
        gate.reshape(E, KC, 128).transpose(2, 1, 0).reshape(128, KC * E))
    BBE = np.ascontiguousarray(
        np.broadcast_to(eb[:, None, :], (E, 128, eb.shape[1])))
    TRIU, RECT, IDN, ONES, SUT, IOTAC, IOTB, EOFF = make_consts(T, QT)
    maps = []
    for k in range(n_cores):
        maps.append({
            "x": np.ascontiguousarray(x[BPC * k:BPC * (k + 1)]),
            "c": np.ascontiguousarray(c[BPC * k:BPC * (k + 1)]),
            "gfmh": gfm, "WT": WT, "BBEh": BBE,
            "wmod": np.ascontiguousarray(wmod),
            "TRIUh": TRIU, "RECTh": RECT, "IDNh": IDN, "ONESh": ONES,
            "SUTh": SUT, "IOTACh": IOTAC, "IOTBh": IOTB, "EOFFh": EOFF,
        })
    return maps


def kernel(L0_attn_x, L0_ffn_gate_w, L0_ffn_expert_w, L0_ffn_expert_b, L0_c,
           L0_wmod):
    x = np.ascontiguousarray(np.asarray(L0_attn_x, dtype=np.float32))
    gate = np.asarray(L0_ffn_gate_w, dtype=np.float32)
    ew = np.asarray(L0_ffn_expert_w, dtype=np.float32)
    eb = np.asarray(L0_ffn_expert_b, dtype=np.float32)
    c = np.asarray(L0_c, dtype=np.float32)
    wmod = np.asarray(L0_wmod, dtype=np.float32)
    T = x.shape[1]
    QT = 1024
    nc = build_nc(T=T, QT=QT)
    maps = make_in_maps(x, gate, ew, eb, c, wmod, T, QT)
    res = run_bass_kernel_spmd(nc, maps, core_ids=list(range(8)))
    return np.concatenate([np.asarray(r["out"]) for r in res.results], axis=0)
